# version 05
# baseline (speedup 1.0000x reference)
import numpy as np
import ml_dtypes
import concourse.bass as bass
import concourse.mybir as mybir
from concourse.bass_utils import run_bass_kernel_spmd

F32 = mybir.dt.float32
BF16 = mybir.dt.bfloat16
I8 = mybir.dt.int8
I32 = mybir.dt.int32
AF = mybir.ActivationFunctionType
ALU = mybir.AluOpType
AX = mybir.AxisListType
BF = ml_dtypes.bfloat16

B, T, HID = 4, 4096, 2048
NH = 8
DK = 128
C = 128
NTC = T // C
NFC = HID // 128
CH = NH * DK
TK = T // 2
INTER = 8192
NI = INTER // 128
SCALE = DK ** -0.5

TAPS = []
PHASES = ["P1", "P2", "P3", "P4"]
NOCONV = False
DBG = False
DBG2 = False
P3LIM = 9


class KB:

    SEMS = ["ds", "dg", "T", "V", "S", "G", "CC"]

    def __init__(self):
        self.marks = {}
        self.real = False
        self.nc = None
        self.t = {}
        self.ps = {}
        self.sem_h = {}
        self.cnt = {}

    def reset_counts(self):
        self.cnt = {s: 0 for s in self.SEMS}

    def do(self, eng, name, *a, sem=None, n=1, **k):
        i = None
        if self.real:
            i = getattr(eng, name)(*a, **k)
        if sem is not None:
            self.cnt[sem] += n
            if self.real:
                i.then_inc(self.sem_h[sem], n)
        return i

    def mark(self, label, sem):
        if not self.real:
            assert label not in self.marks, f"dup mark {label}"
            self.marks[label] = (sem, self.cnt[sem])

    def wait(self, eng, *labels):
        if self.real:
            for label in labels:
                sem, v = self.marks[label]
                if v > 0:
                    eng.wait_ge(self.sem_h[sem], v)


def build_nc(taps=(), phases=("P1", "P2", "P3", "P4")):
    nc = bass.Bass("TRN2", target_bir_lowering=False,
                   detect_race_conditions=False)
    kb = KB()
    kb.nc = nc
    kb.phases = set(phases)
    t = kb.t

    def ein(n, s, d):
        h = nc.dram_tensor(n, list(s), d, kind="ExternalInput")
        t[n] = h
        return h

    ein("hT", [HID, T], F32)
    ein("hTres", [HID, TK], F32)
    for w in ["Wq", "Wk", "Wv", "Ww"]:
        ein(w, [HID, CH], F32)
    ein("cw", [128, 128], F32)
    ein("Wf1", [HID, 128], F32)
    ein("Wf2", [128, CH], F32)
    ein("Wb", [HID, NH], F32)
    ein("Wg1", [HID, 128], F32)
    ein("Wg2", [128, CH], F32)
    ein("bg2", [1, CH], F32)
    ein("norm_w", [128, 1], F32)
    ein("Wo", [16, 128, HID], F32)
    ein("wg8", [NI, NFC, 128, 128], I8)
    ein("wu8", [NI, NFC, 128, 128], I8)
    ein("wd8", [NFC, NI, 128, 128], I8)
    ein("s_gate", [128, NI], F32)
    ein("s_up", [128, NI], F32)
    ein("s_down", [128, NFC], F32)
    ein("ident", [128, 128], BF16)
    ein("mcs", [128, 128], BF16)
    ein("mcols", [128, 2], BF16)
    ein("mask", [128, 128], F32)
    ein("meta", [1, 64], I32)

    t["outT"] = nc.dram_tensor("outT", [HID, TK], F32, kind="ExternalOutput")

    def din(n, s, d):
        t[n] = nc.dram_tensor(n, list(s), d, kind="Internal")

    for n in ["sp_q", "sp_k", "sp_v", "sp_w"]:
        din(n, [NTC, CH, C], BF16)
    din("sp_ghi", [NTC, C, CH], BF16)
    din("sp_glo", [NTC, C, CH], BF16)
    din("sp_sg", [NTC, C, CH], BF16)
    din("sp_x1", [NFC, 128, TK], BF16)
    din("ost", [2 * 1024, 2048], BF16)
    din("cc", [2, 2048, 2048], BF16)
    din("ccP", [2048, 2048], BF16)
    din("fuse", [NI, 128, TK], BF16)
    for tap in taps:
        th = t[tap]
        t["tap_" + tap] = nc.dram_tensor("tap_" + tap, list(th.shape),
                                         th.dtype, kind="ExternalOutput")
    if DBG:
        t["dbg_ps"] = nc.dram_tensor("dbg_ps", [128, 512], F32,
                                     kind="ExternalOutput")
        t["dbg_stg"] = nc.dram_tensor("dbg_stg", [128, 512], BF16,
                                      kind="ExternalOutput")
    if DBG2:
        for nm2, dt2 in [("vt_s", BF16), ("og_b", BF16), ("ro8", F32),
                         ("orms8", F32),
                         ("ghc0", BF16), ("glc0", BF16), ("qcb0", BF16),
                         ("D_s", F32), ("E_s", F32), ("U_f", F32),
                         ("o_s", F32), ("skf_s", BF16), ("wt_s", BF16),
                         ("kqm_s", BF16), ("ccol", F32), ("comb_s", F32),
                         ("es_s", F32), ("s_s", F32), ("at_s", BF16),
                         ("SvF_b", BF16), ("pmt_s", BF16)]:
            shp = [128, 16 if nm2 == "ccol" else
                   8 if nm2 in ("comb_s", "ro8", "orms8") else
                   512 if nm2 in ("kqm_s", "es_s", "s_s", "at_s", "pmt_s") else 1024]
            t["dbg2_" + nm2] = nc.dram_tensor("dbg2_" + nm2, shp, dt2,
                                              kind="ExternalOutput")

    SB_BASE = 20480
    SB_TOP = 224 * 1024 - 64
    SZ = {BF16: 2, F32: 4, I8: 1, I32: 4}
    off = [SB_BASE]

    def sb(name, shape, dtype, at=None):
        per = int(np.prod(shape[1:])) * SZ[dtype]
        if at is None:
            at = off[0]
            off[0] = at + ((per + 31) // 32 * 32)
        else:
            at = ((at + 31) // 32) * 32
        assert at >= SB_BASE and at + per <= SB_TOP, (name, at, per)
        t[name] = nc.alloc_sbuf_tensor_at(name, list(shape), dtype, offset=at)
        return at + per

    t["meta_s"] = nc.alloc_sbuf_tensor("meta_s", [1, 64], I32)
    sb("ident_s", [128, 128], BF16)
    sb("mcs_s", [128, 128], BF16)
    sb("mcols_s", [128, 2], BF16)
    sb("mask_s", [128, 128], F32)
    sb("ones_r", [1, 128], BF16)
    sb("ones_c", [128, 1], BF16)
    sb("ones_r32", [1, 128], F32)
    sb("wrm", [1, 8], F32)
    sb("cw_s", [128, 128], F32)
    sb("wbS", [128, NFC * 8], BF16)
    sb("beta_t", [128, NTC * NH], BF16)
    sb("nw_s", [128, 1], F32)
    sb("f1_s", [128, T], BF16)
    sb("f1g_s", [128, T], BF16)
    sb("wf2_s", [128, CH], BF16)
    sb("wg2_s", [128, CH], BF16)
    sb("bg2_s", [1, CH], BF16)
    sb("sdown_s", [128, NFC], F32)
    sb("eps6", [128, 1], F32)
    sb("eps5", [128, 1], F32)
    sb("SkT", [128, NH * 128], F32)
    sb("Sv", [128, NH * 128], F32)
    sb("pcol", [128, NH], F32)
    ARENA = off[0]

    def ov(name, shape, dtype, rel):
        return sb(name, shape, dtype, at=ARENA + rel) - ARENA

    a = ov("hT_s", [128, NFC * T], BF16, 0)
    a = ov("wbufA", [128, NFC * 128], BF16, a)
    a = ov("wbufB", [128, NFC * 128], BF16, a)
    a = ov("cbuf", [128, 515], F32, a)
    a = ov("ybuf", [128, 512], F32, a)
    a = ov("stgA", [128, 512], BF16, a)
    a = ov("stgB", [128, 512], BF16, a)
    zoff = a
    a = ov("zbuf", [128, 1024], F32, a)
    a = ov("gbuf", [128, 1024], F32, a)
    a = ov("hibuf", [128, 1024], BF16, a)
    a = ov("lobuf", [128, 1024], BF16, a)
    a = ov("sgbuf", [128, 1024], BF16, a)
    boff = a
    a = ov("betaT", [8, T], BF16, a)
    ov("hstgA", [128, 2048], F32, zoff)
    ov("wstgA", [128, NFC * 128], F32, zoff)
    ov("hstgB", [128, 2048], F32, boff)
    ov("wstgB", [128, NFC * 128], F32, boff)
    if DBG:
        a = ov("dbgv_s", [128, 512], F32, a)

    a2 = 0
    for nm2 in ["qcb", "kcb", "vcb", "wcb", "ghc", "glc", "sgc"]:
        for par in (0, 1):
            a2 = ov(f"{nm2}{par}", [128, 1024], BF16, a2)
    for nm2, dt2 in [("D_s", F32), ("E_s", F32), ("U_f", F32), ("SkF_f", F32),
                     ("SvF_f", F32), ("scrF", F32), ("o_s", F32),
                     ("wt_s", BF16), ("um_s", BF16), ("kt_s", BF16),
                     ("vt_s", BF16), ("skf_s", BF16), ("U_b", BF16),
                     ("SkF_b", BF16), ("SvF_b", BF16), ("og_b", BF16),
                     ("ot0", BF16), ("ot1", BF16)]:
        a2 = ov(nm2, [128, 1024], dt2, a2)
    ov("wo_s", [128, 16 * 2048], BF16, 0)
    ov("x1a", [128, NFC * TK], BF16, 0)
    ov("fusS", [128, NI * 512], BF16, 0)
    ov("oF0", [128, 16 * 512], BF16, 64 * 1024)
    ov("wd_s", [128, NI * 128], BF16, 64 * 1024)
    ov("oF1", [128, 16 * 512], BF16, 80 * 1024)
    ov("wg0", [128, NFC * 128], BF16, 80 * 1024)
    ov("wg1_", [128, NFC * 128], BF16, 84 * 1024)
    ov("wu0", [128, NFC * 128], BF16, 88 * 1024)
    ov("wu1_", [128, NFC * 128], BF16, 92 * 1024)
    ov("x1r0", [128, NFC * 512], BF16, 80 * 1024)
    ov("hres0", [128, NFC * 512], BF16, 96 * 1024)
    ov("x1r1", [128, NFC * 512], BF16, 96 * 1024)
    ov("fstg0", [128, 2048], BF16, 96 * 1024)
    ov("fstg1", [128, 2048], BF16, 100 * 1024)
    ov("gsil", [128, 2048], F32, 104 * 1024)
    a3 = ov("r_f", [128, NFC * 512], F32, 112 * 1024)
    a3 = ov("rb", [128, NFC * 512], BF16, a3)
    ov("x1t", [128, NFC * 512], BF16, 144 * 1024)
    a3 = ov("sqb", [128, 512], BF16, a3)
    a3 = ov("sg8", [128, NI], F32, a3)
    a3 = ov("su8", [128, NI], F32, a3)
    a3 = ov("sd8", [128, NFC], F32, a3)
    a3 = ov("musb", [1, 512], F32, a3)
    a3 = ov("sqsb", [1, 512], F32, a3)
    a3 = ov("var_", [1, 512], F32, a3)
    a3 = ov("rstd", [1, 512], F32, a3)
    for nm2, w2, dt2 in [("s_s", 512, F32), ("sm_s", 512, F32),
                         ("es_s", 512, F32), ("af_s", 512, F32),
                         ("kqm_s", 512, BF16), ("ab_s", 512, BF16),
                         ("at_s", 512, BF16), ("pmt_s", 512, BF16),
                         ("ccol", 16, F32), ("tmp8", 8, F32), ("comb_s", 8, F32),
                         ("ssq8", 8, F32), ("wr8", 8, F32), ("rb8", 8, F32),
                         ("nm8", 8, F32), ("ssm8", 8, F32), ("rcp8", 8, F32),
                         ("orms8", 8, F32), ("ro8", 8, F32)]:
        a2 = ov(nm2, [128, w2], dt2, a2)

    for nm, shape, dt_ in [("psA0", [128, 512], F32), ("psA1", [128, 512], F32),
                           ("psA2", [128, 512], F32), ("psA3", [128, 512], F32),
                           ("psB", [128, 1024], F32), ("psT", [128, 1024], BF16),
                           ("psC", [128, 512], F32)]:
        kb.ps[nm] = nc.alloc_psum_tensor(nm, shape, dt_)

    from contextlib import ExitStack
    with ExitStack() as ctx:
        for s in KB.SEMS:
            kb.sem_h[s] = ctx.enter_context(nc.semaphore("sem_" + s))
        kb.taps = list(taps)

        def emit_all():
            kb.reset_counts()
            E = {"sync": None, "g": None, "t": None, "v": None, "s": None}
            if kb.real:
                nums = sorted(h.num for h in kb.sem_h.values())
                assert nums == list(range(nums[0], nums[0] + len(nums)))
                rng = range(nums[0], nums[-1] + 1)
                nc.gpsimd.dma_reset(rng)
                nc.gpsimd.sem_clear(rng)
                nc.all_engine_barrier()
                E = {"sync": nc.sync, "g": nc.gpsimd, "t": nc.tensor,
                     "v": nc.vector, "s": nc.scalar}
            emit_sync(kb, E["sync"])
            emit_gpsimd(kb, E["g"])
            emit_tensor(kb, E["t"])
            emit_vector(kb, E["v"])
            emit_scalar(kb, E["s"])

        kb.real = False
        emit_all()
        cnt1 = dict(kb.cnt)
        kb.real = True
        emit_all()
        assert kb.cnt == cnt1, (kb.cnt, cnt1)
    return nc


P1W = ["sp_q", "sp_k", "sp_v", "sp_w"]
WSRC = {"sp_q": "Wq", "sp_k": "Wk", "sp_v": "Wv", "sp_w": "Ww"}


def p1_gpsimd(kb, g):
    t = kb.t
    D = lambda *a, **k: kb.do(g, "dma_start", *a, sem="dg", n=16, **k)
    D(t["wf2_s"][:, :], t["Wf2"].ap()[:, :])
    D(t["wg2_s"][:, :], t["Wg2"].ap()[:, :])
    D(t["bg2_s"][:, :], t["bg2"].ap()[:, :])
    D(t["wbS"][:, :].rearrange("p (a b) -> p a b", b=8),
      bass.AP(t["Wb"], 0, [[NH, 128], [NH * 128, NFC], [1, NH]]))
    kb.do(g, "memset", t["ones_r"][:, :], 1.0)
    kb.do(g, "memset", t["ones_c"][:, :], 1.0)
    kb.do(g, "memset", t["ones_r32"][:, :], 1.0)
    kb.do(g, "memset", t["wrm"][:, :], 1.0)
    kb.do(g, "memset", t["eps6"][:, :], 1e-6)
    kb.do(g, "memset", t["eps5"][:, :], 1e-5)
    kb.do(g, "memset", t["SkT"][:, :], 0.0)
    kb.do(g, "memset", t["Sv"][:, :], 0.0)
    kb.do(g, "memset", t["pcol"][:, :], 0.0, sem="G")
    kb.mark("g.consts", "dg")
    kb.mark("g.memset", "G")
    for fh in range(0, 2 * NFC, 2):
        fc, hf = fh // 2, fh % 2
        if fh >= 2:
            kb.wait(g, f"v.hcast.{fh - 2}")
        D(t["hstgA"][:, :], t["hT"].ap()[fc * 128:(fc + 1) * 128,
                                         hf * 2048:(hf + 1) * 2048])
        D(t["meta_s"][:, :], t["meta"].ap()[:, :])
        kb.mark(f"g.hT.{fh}", "dg")
    kb.mark("g.hTsent", "dg")
    for wi in range(4):
        W = t[WSRC[P1W[wi]]]
        for j in range(8):
            wj = wi * 8 + j
            stg = t["wstgA"] if wj % 2 == 0 else t["wstgB"]
            if wj >= 2:
                kb.wait(g, f"v.wcast.{wj - 2}")
            else:
                kb.wait(g, f"v.hcast.{30 + wj}")
            D(stg[:, :].rearrange("p (a b) -> p a b", b=128),
              bass.AP(W, j * 128, [[CH, 128], [128 * CH, NFC], [1, 128]]))
            D(t["meta_s"][:, :], t["meta"].ap()[:, :])
            kb.mark(f"g.p1w.{wj}", "dg")
    kb.mark("g.wsent", "dg")
    D(t["meta_s"][:, :], t["meta"].ap()[:, :])
    kb.mark("g.wsent2", "dg")
    kb.wait(g, "v.wcast.30")
    D(t["wstgA"][:, 0:2048].rearrange("p (a b) -> p a b", b=128),
      bass.AP(t["Wf1"], 0, [[128, 128], [128 * 128, NFC], [1, 128]]))
    kb.mark("g.wf1", "dg")
    kb.wait(g, "v.wcast.31")
    D(t["wstgB"][:, 0:2048].rearrange("p (a b) -> p a b", b=128),
      bass.AP(t["Wg1"], 0, [[128, 128], [128 * 128, NFC], [1, 128]]))
    kb.mark("g.wg1", "dg")
    D(t["meta_s"][:, :], t["meta"].ap()[:, :])
    kb.mark("g.gsent", "dg")
    kb.mark("p1.dg", "dg")
    if DBG:
        kb.wait(g, "v.dbg")
        D(t["dbg_ps"].ap(), t["dbgv_s"][:, :])


def p1_sync(kb, s):
    t = kb.t
    D = lambda *a, **k: kb.do(s, "dma_start", *a, sem="ds", n=16, **k)
    D(t["ident_s"][:, :], t["ident"].ap()[:, :])
    D(t["mcs_s"][:, :], t["mcs"].ap()[:, :])
    D(t["mcols_s"][:, :], t["mcols"].ap()[:, :])
    D(t["mask_s"][:, :], t["mask"].ap()[:, :])
    D(t["meta_s"][:, :], t["meta"].ap()[:, :])
    D(t["cw_s"][:, :], t["cw"].ap()[:, :])
    D(t["nw_s"][:, :], t["norm_w"].ap()[:, :])
    kb.mark("s.consts", "ds")
    for fh in range(1, 2 * NFC, 2):
        fc, hf = fh // 2, fh % 2
        if fh >= 3:
            kb.wait(s, f"v.hcast.{fh - 2}")
        D(t["hstgB"][:, :], t["hT"].ap()[fc * 128:(fc + 1) * 128,
                                         hf * 2048:(hf + 1) * 2048])
        D(t["meta_s"][:, :], t["meta"].ap()[:, :])
        kb.mark(f"s.hT.{fh}", "ds")
    for wi in range(4):
        sp = t[P1W[wi]]
        for j in range(8):
            for tau in range(8):
                idx = (wi * 8 + j) * 8 + tau
                stg = t["stgA"] if idx % 2 == 0 else t["stgB"]
                kb.wait(s, f"sc.p1s.{idx}")
                dst = bass.AP(sp, (tau * 4) * (CH * C) + (j * 128) * C,
                              [[C, 128], [CH * C, 4], [1, C]])
                D(dst, stg[:, :].rearrange("p (a b) -> p a b", b=C))
                kb.mark(f"s.p1s.{idx}", "ds")
    for ts in range(NTC):
        kb.wait(s, f"v.glo.{ts}")
        D(bass.AP(t["sp_ghi"], ts * C * CH, [[CH, C], [1, CH]]), t["hibuf"][:, :])
        D(bass.AP(t["sp_glo"], ts * C * CH, [[CH, C], [1, CH]]), t["lobuf"][:, :])
        kb.mark(f"s.g.{ts}", "ds")
    for ts in range(NTC):
        kb.wait(s, f"sc.sgz.{ts}")
        D(bass.AP(t["sp_sg"], ts * C * CH, [[CH, C], [1, CH]]), t["sgbuf"][:, :])
        kb.mark(f"s.sg.{ts}", "ds")
    kb.mark("p1.ds", "ds")
    if DBG:
        kb.wait(s, "sc.p1s.0")
        D(t["dbg_stg"].ap(), t["stgA"][:, :])


def p1_tensor(kb, e):
    t, ps = kb.t, kb.ps
    MM = lambda *a, **k: kb.do(e, "matmul", *a, sem="T", **k)
    kb.wait(e, "g.consts", "s.consts", "g.memset")
    kb.wait(e, "g.consts", "s.consts")
    for _ in range(144):
        MM(ps["psB"][:, 0:128], t["ident_s"][:, :], t["ident_s"][:, 0:128],
           start=True, stop=True)
    kb.mark("t.warm", "T")
    kb.wait(e, "v.hcast.30", "v.hcast.31")
    for wi in range(4):
        for j in range(8):
            wj = wi * 8 + j
            buf = t["wbufA"] if wj % 2 == 0 else t["wbufB"]
            kb.wait(e, f"v.wcast.{wj}")
            for tau in range(8):
                idx = wj * 8 + tau
                pb = ps[f"psA{idx % 4}"]
                if idx >= 4:
                    kb.wait(e, f"v.p1c.{idx - 4}")
                for fc in range(NFC):
                    MM(pb[:, :], buf[:, fc * 128:(fc + 1) * 128],
                       t["hT_s"][:, fc * T + tau * 512: fc * T + tau * 512 + 512],
                       start=(fc == 0), stop=(fc == NFC - 1))
                kb.mark(f"t.p1m.{idx}", "T")
            kb.mark(f"t.p1w.{wj}", "T")
    kb.wait(e, "v.p1c.last")
    for src, bufn in [(0, "wbufA"), (1, "wbufB")]:
        kb.wait(e, "v.wcast.f" if src == 0 else "v.wcast.g")
        kb.wait(e, "g.wf1" if src == 0 else "g.wg1")
        for tau in range(8):
            fidx = src * 8 + tau
            pb = ps[f"psA{fidx % 2}"]
            if fidx >= 2:
                kb.wait(e, f"v.f1e.{fidx - 2}")
            for fc in range(NFC):
                MM(pb[:, :], t[bufn][:, fc * 128:(fc + 1) * 128],
                   t["hT_s"][:, fc * T + tau * 512: fc * T + tau * 512 + 512],
                   start=(fc == 0), stop=(fc == NFC - 1))
            kb.mark(f"t.f1m.{fidx}", "T")
    kb.wait(e, "v.f1e.15")
    for ts in range(NTC):
        if ts >= 1:
            kb.wait(e, f"sc.gz.{ts - 1}")
        for half in range(2):
            MM(ps["psB"][:, half * 512:(half + 1) * 512],
               t["f1_s"][:, ts * 128:(ts + 1) * 128],
               t["wf2_s"][:, half * 512:(half + 1) * 512],
               start=True, stop=True)
        kb.mark(f"t.gz.{ts}", "T")
    kb.wait(e, "v.f1e.14", "v.f1e.15")
    for tau in range(8):
        if tau >= 2:
            kb.wait(e, f"sc.beta.{tau - 2}")
        pb = ps[f"psA{tau % 2}"]
        for fc in range(NFC):
            MM(pb[0:8, :], t["wbS"][:, fc * 8:(fc + 1) * 8],
               t["hT_s"][:, fc * T + tau * 512: fc * T + tau * 512 + 512],
               start=(fc == 0), stop=(fc == NFC - 1))
        kb.mark(f"t.beta.{tau}", "T")
    kb.wait(e, "sc.gz.31")
    for ts in range(NTC):
        if ts >= 1:
            kb.wait(e, f"sc.sgz.{ts - 1}")
        for half in range(2):
            MM(ps["psB"][:, half * 512:(half + 1) * 512],
               t["f1g_s"][:, ts * 128:(ts + 1) * 128],
               t["wg2_s"][:, half * 512:(half + 1) * 512],
               start=True, stop=False)
            MM(ps["psB"][:, half * 512:(half + 1) * 512],
               t["ones_r"][:, 0:128],
               t["bg2_s"][:, half * 512:(half + 1) * 512],
               start=False, stop=True)
        kb.mark(f"t.sgz.{ts}", "T")
    kb.wait(e, "sc.beta.7")
    for ts in range(NTC):
        if ts % 8 == 0 and ts > 0:
            kb.wait(e, f"v.betat.{ts // 8 - 1}")
        kb.do(e, "transpose", ps["psT"][:, (ts % 8) * 8:(ts % 8) * 8 + 8],
              t["betaT"][:, ts * 128:(ts + 1) * 128], t["ident_s"][0:8, 0:8],
              sem="T")
        if ts % 8 == 7:
            kb.mark(f"t.betat.{ts // 8}", "T")
    kb.mark("p1.T", "T")


def p1_casts(kb, e):
    t = kb.t
    for fh in range(2 * NFC):
        fc, hf = fh // 2, fh % 2
        stg = t["hstgA"] if fh % 2 == 0 else t["hstgB"]
        kb.wait(e, f"g.hT.{fh}" if fh % 2 == 0 else f"s.hT.{fh}")
        kb.do(e, "tensor_copy",
              t["hT_s"][:, fc * T + hf * 2048: fc * T + hf * 2048 + 2048],
              stg[:, :], sem="V")
        kb.mark(f"v.hcast.{fh}", "V")


def p1_vector(kb, e):
    t, ps = kb.t, kb.ps
    kb.wait(e, "s.consts")
    p1_casts(kb, e)
    if DBG:
        kb.wait(e, "t.p1m.0")
        kb.do(e, "tensor_copy", t["dbgv_s"][:, :], ps["psA0"][:, :], sem="V")
        kb.mark("v.dbg", "V")
    for wi in range(4):
        for j in range(8):
            wj = wi * 8 + j
            stg = t["wstgA"] if wj % 2 == 0 else t["wstgB"]
            buf = t["wbufA"] if wj % 2 == 0 else t["wbufB"]
            kb.wait(e, f"g.p1w.{wj}")
            if wj >= 2:
                kb.wait(e, f"t.p1w.{wj - 2}")
            kb.do(e, "tensor_copy", buf[:, :], stg[:, :], sem="V")
            kb.mark(f"v.wcast.{wj}", "V")
            for tau in range(8):
                idx = wj * 8 + tau
                pb = ps[f"psA{idx % 4}"]
                kb.wait(e, f"t.p1m.{idx}")
                if idx >= 1:
                    kb.wait(e, f"sc.p1s.{idx - 1}")
                if NOCONV:
                    kb.wait(e, f"sc.p1s.{idx}")
                    kb.do(e, "engine_nop", sem="V")
                    kb.mark(f"v.p1c.{idx}", "V")
                    continue
                if tau == 0:
                    kb.do(e, "memset", t["cbuf"][:, 0:3], 0.0)
                kb.do(e, "tensor_copy", t["cbuf"][:, 3:515], pb[:, :])
                cw = t["cw_s"]
                cwb = wi * 32 + j * 4
                kb.do(e, "tensor_scalar_mul", t["ybuf"][:, :],
                      t["cbuf"][:, 3:515], cw[:, cwb + 3:cwb + 4])
                for i in (2, 1, 0):
                    kb.do(e, "scalar_tensor_tensor", t["ybuf"][:, :],
                          t["cbuf"][:, i:i + 512], cw[:, cwb + i:cwb + i + 1],
                          t["ybuf"][:, :], ALU.mult, ALU.add)
                kb.do(e, "tensor_copy", t["cbuf"][:, 0:3], t["cbuf"][:, 512:515],
                      sem="V")
                kb.mark(f"v.p1c.{idx}", "V")
    kb.mark("v.p1c.last", "V")
    for src, bufn, lbl in [(0, "wbufA", "g.wg1"), (1, "wbufB", "g.gsent")]:
        kb.wait(e, lbl)
        kb.wait(e, "v.p1c.last")
        stg = t["wstgA"] if src == 0 else t["wstgB"]
        kb.do(e, "tensor_copy", t[bufn][:, 0:2048], stg[:, 0:2048], sem="V")
        kb.mark("v.wcast.f" if src == 0 else "v.wcast.g", "V")
    for src, outn in [(0, "f1_s"), (1, "f1g_s")]:
        for tau in range(8):
            fidx = src * 8 + tau
            pb = ps[f"psA{fidx % 2}"]
            kb.wait(e, f"t.f1m.{fidx}")
            kb.do(e, "tensor_copy", t[outn][:, tau * 512:(tau + 1) * 512],
                  pb[:, :], sem="V")
            kb.mark(f"v.f1e.{fidx}", "V")
    for ts in range(NTC):
        kb.wait(e, f"sc.gn.{ts}")
        if ts >= 1:
            kb.wait(e, f"s.g.{ts - 1}")
        kb.do(e, "tensor_copy", t["hibuf"][:, :], t["gbuf"][:, :], sem="V")
        kb.mark(f"v.ghi.{ts}", "V")
        kb.wait(e, f"v.ghi.{ts}")
        kb.do(e, "tensor_tensor", t["lobuf"][:, :], t["gbuf"][:, :],
              t["hibuf"][:, :], ALU.subtract, sem="V")
        kb.mark(f"v.glo.{ts}", "V")
    for g8 in range(NTC // 8):
        kb.wait(e, f"t.betat.{g8}")
        kb.do(e, "tensor_copy", t["beta_t"][:, g8 * 64:(g8 + 1) * 64],
              ps["psT"][:, 0:64], sem="V")
        kb.mark(f"v.betat.{g8}", "V")
    kb.mark("p1.V", "V")


def load_act_set(kb, e, idx):
    if kb.real:
        e.add_instruction(mybir.InstLoadActFuncSet(
            name=kb.nc.get_next_instruction_name(), act_func_set_id=idx,
            ins=[], outs=[]))


def act_warmup(kb, e):
    t = kb.t
    kb.wait(e, "g.memset")
    for idx, fn in [(6, AF.Exp), (2, AF.Sigmoid), (18, AF.Silu)]:
        load_act_set(kb, e, idx)
        kb.do(e, "activation", t["wrm"][0:1, 0:8], t["wrm"][0:1, 0:8], fn)


def p1_scalar(kb, e):
    t, ps = kb.t, kb.ps
    act_warmup(kb, e)
    for wi in range(4):
        for j in range(8):
            for tau in range(8):
                idx = (wi * 8 + j) * 8 + tau
                stg = t["stgA"] if idx % 2 == 0 else t["stgB"]
                pb = ps[f"psA{idx % 4}"]
                if NOCONV:
                    kb.wait(e, f"t.p1m.{idx}")
                else:
                    kb.wait(e, f"v.p1c.{idx}")
                if idx >= 2:
                    kb.wait(e, f"s.p1s.{idx - 2}")
                kb.do(e, "activation", stg[:, :],
                      pb[:, :] if NOCONV else t["ybuf"][:, :],
                      AF.Identity if NOCONV else AF.Silu,
                      sem="S")
                kb.mark(f"sc.p1s.{idx}", "S")
    kb.wait(e, "v.wcast.f", "v.wcast.g")
    load_act_set(kb, e, 6)
    kb.do(e, "activation", t["wrm"][0:1, 0:8], t["wrm"][0:1, 0:8], AF.Exp)
    for ts in range(NTC):
        kb.wait(e, f"t.gz.{ts}")
        kb.do(e, "activation", t["zbuf"][:, :], ps["psB"][:, :], AF.Exp,
              scale=-1.0, sem="S")
        kb.mark(f"sc.gz.{ts}", "S")
        if ts >= 1:
            kb.wait(e, f"v.glo.{ts - 1}")
        kb.do(e, "activation", t["gbuf"][:, :], t["zbuf"][:, :], AF.Ln,
              bias=1.0, sem="S")
        kb.mark(f"sc.gn.{ts}", "S")
    load_act_set(kb, e, 2)
    kb.do(e, "activation", t["wrm"][0:1, 0:8], t["wrm"][0:1, 0:8], AF.Sigmoid)
    for tau in range(8):
        pb = ps[f"psA{tau % 2}"]
        kb.wait(e, f"t.beta.{tau}")
        kb.do(e, "activation", t["betaT"][:, tau * 512:(tau + 1) * 512],
              pb[0:8, :], AF.Sigmoid, sem="S")
        kb.mark(f"sc.beta.{tau}", "S")
    for ts in range(NTC):
        kb.wait(e, f"t.sgz.{ts}")
        if ts >= 1:
            kb.wait(e, f"s.sg.{ts - 1}")
        kb.do(e, "activation", t["sgbuf"][:, :], ps["psB"][:, :], AF.Sigmoid,
              sem="S")
        kb.mark(f"sc.sgz.{ts}", "S")
        if ts == NTC - 1:
            kb.mark("p1.S", "S")



SPF = {"sp_q": "qcb", "sp_k": "kcb", "sp_v": "vcb", "sp_w": "wcb"}


def p2_sync(kb, s):
    t = kb.t
    D = lambda *a, **k: kb.do(s, "dma_start", *a, sem="ds", n=16, **k)
    kb.wait(s, "p1.T", "p1.V", "p1.S", "p1.ds", "p1.dg")
    for tau in range(NTC):
        p = tau % 2
        if tau >= 2:
            kb.wait(s, f"c{tau - 2}.tlast", f"c{tau - 2}.vlast")
        for spn, bufn in SPF.items():
            D(t[f"{bufn}{p}"][:, :].rearrange("p (h u) -> p h u", u=C),
              bass.AP(t[spn], tau * CH * C, [[C, 128], [128 * C, NH], [1, C]]))
        D(t[f"ghc{p}"][:, :], bass.AP(t["sp_ghi"], tau * C * CH, [[CH, C], [1, CH]]))
        D(t[f"glc{p}"][:, :], bass.AP(t["sp_glo"], tau * C * CH, [[CH, C], [1, CH]]))
        D(t[f"sgc{p}"][:, :], bass.AP(t["sp_sg"], tau * C * CH, [[CH, C], [1, CH]]))
        D(t["meta_s"][:, :], t["meta"].ap()[:, :])
        kb.mark(f"c{tau}.ld", "ds")
        if DBG2 and tau == 0:
            kb.wait(s, "c0.vlast", "c0.g1.st")
            for nm2 in ["vt_s", "og_b", "ro8", "orms8",
                        "ghc0", "glc0", "qcb0",
                        "D_s", "E_s", "U_f", "o_s", "skf_s", "wt_s", "kqm_s",
                        "ccol", "comb_s", "es_s", "s_s", "at_s", "SvF_b",
                        "pmt_s"]:
                D(t["dbg2_" + nm2].ap()[:, :], t[nm2][:, :])
            kb.mark("dbg2", "ds")
    kb.mark("p2.end.sync", "ds")


def p2_gpsimd(kb, g):
    t = kb.t
    kb.wait(g, "p1.T", "p1.V", "p1.S", "p1.ds", "p1.dg")
    for tau in range(NTC):
        kb.wait(g, f"c{tau}.ote")
        off = (tau // 16) * (1024 * 2048) + (tau % 16) * 128
        kb.do(g, "dma_start",
              bass.AP(t["ost"], off, [[2048, 128], [128 * 2048, NH], [1, 128]]),
              t[f"ot{tau % 2}"][:, :].rearrange("p (h u) -> p h u", u=128),
              sem="dg", n=16)
        kb.mark(f"c{tau}.ost", "dg")
        if tau == 15:
            kb.wait(g, "c15.ost")
            kb.do(g, "collective_compute", "AllGather", ALU.bypass,
                  [[0, 1], [2, 3], [4, 5], [6, 7]],
                  [bass.AP(t["ost"], 0, [[2048, 1024], [1, 2048]])],
                  [bass.AP(t["cc"], 0, [[2048, 2048], [1, 2048]])], sem="CC")
            kb.mark("ag.a", "CC")
        if tau == 31:
            kb.wait(g, "c31.ost")
            kb.do(g, "collective_compute", "AllGather", ALU.bypass,
                  [[0, 1], [2, 3], [4, 5], [6, 7]],
                  [bass.AP(t["ost"], 1024 * 2048, [[2048, 1024], [1, 2048]])],
                  [bass.AP(t["cc"], 2048 * 2048, [[2048, 2048], [1, 2048]])],
                  sem="CC")
            kb.mark("ag.b", "CC")
    kb.wait(g, "ag.a", "ag.b")
    if kb.real:
        reg = g.alloc_register("r_ccp")
        g.reg_load(reg, t["meta_s"][0:1, 1:2])
        g.dma_start(bass.AP(t["ccP"], 0, [[2048, 2048], [1, 2048]]),
                    bass.AP(t["cc"], reg, [[2048, 2048], [1, 2048]])
                    ).then_inc(kb.sem_h["dg"], 16)
    kb.cnt["dg"] += 16
    kb.mark("ccp", "dg")


def p2_tensor(kb, e):
    t, ps = kb.t, kb.ps
    MM = lambda *a, **k: kb.do(e, "matmul", *a, sem="T", **k)
    TR = lambda o, i: kb.do(e, "transpose", o, i, t["ident_s"][:, :], sem="T")
    kb.wait(e, "p1.T", "p1.V", "p1.S", "p1.ds", "p1.dg")
    for tau in range(NTC):
        p = tau % 2
        gh, gl = t[f"ghc{p}"], t[f"glc{p}"]
        kb.wait(e, f"c{tau}.ld")
        if DBG2 and tau == 1:
            kb.wait(e, "dbg2")
        if tau >= 1:
            kb.wait(e, f"c{tau - 1}.g1.st")
        for half in range(2):
            MM(ps["psB"][:, half * 512:(half + 1) * 512], t["mcs_s"][:, :],
               gh[:, half * 512:(half + 1) * 512], start=True, stop=False)
            MM(ps["psB"][:, half * 512:(half + 1) * 512], t["mcs_s"][:, :],
               gl[:, half * 512:(half + 1) * 512], start=False, stop=True)
        kb.mark(f"c{tau}.cs", "T")
        if tau >= 1:
            kb.wait(e, f"c{tau - 1}.ccol")
        for h in range(NH):
            MM(ps["psC"][:, h * 2:h * 2 + 2], gh[:, h * 128:(h + 1) * 128],
               t["mcols_s"][:, :], start=True, stop=False)
            MM(ps["psC"][:, h * 2:h * 2 + 2], gl[:, h * 128:(h + 1) * 128],
               t["mcols_s"][:, :], start=False, stop=True)
        kb.mark(f"c{tau}.cols", "T")
        if tau >= 1:
            kb.wait(e, f"c{tau - 1}.ote")
        for h in range(NH):
            TR(ps["psT"][:, h * 128:(h + 1) * 128], t[f"wcb{p}"][:, h * 128:(h + 1) * 128])
        kb.mark(f"c{tau}.r1", "T")
        kb.wait(e, f"c{tau}.r1e", f"c{tau}.Ub")
        for h in range(NH):
            TR(ps["psT"][:, h * 128:(h + 1) * 128], t["U_b"][:, h * 128:(h + 1) * 128])
        kb.mark(f"c{tau}.r2", "T")
        kb.wait(e, f"c{tau}.r2e")
        for h in range(NH):
            TR(ps["psT"][:, h * 128:(h + 1) * 128], t[f"kcb{p}"][:, h * 128:(h + 1) * 128])
        kb.mark(f"c{tau}.r3", "T")
        kb.wait(e, f"c{tau}.r3e")
        for h in range(NH):
            TR(ps["psT"][:, h * 128:(h + 1) * 128], t[f"vcb{p}"][:, h * 128:(h + 1) * 128])
        kb.mark(f"c{tau}.r4", "T")
        kb.wait(e, f"c{tau}.r4e", f"c{tau}.SkFb")
        for h in range(NH):
            TR(ps["psT"][:, h * 128:(h + 1) * 128], t["SkF_b"][:, h * 128:(h + 1) * 128])
        kb.mark(f"c{tau}.r5", "T")
        for G in range(2):
            hs = range(G * 4, G * 4 + 4)
            if (tau, G) != (0, 0):
                pt, pg = (tau, 0) if G == 1 else (tau - 1, 1)
                kb.wait(e, f"c{pt}.g{pg}.kqme")
            for h in hs:
                MM(ps["psA0"][:, (h % 4) * 128:(h % 4 + 1) * 128],
                   t[f"kcb{p}"][:, h * 128:(h + 1) * 128],
                   t[f"qcb{p}"][:, h * 128:(h + 1) * 128], start=True, stop=True)
            kb.mark(f"c{tau}.g{G}.kq", "T")
            if (tau, G) != (0, 0):
                pt, pg = (tau, 0) if G == 1 else (tau - 1, 1)
                kb.wait(e, f"c{pt}.g{pg}.sev")
            kb.wait(e, f"c{tau}.r5e" if G == 0 else f"c{tau}.r5e",
                    f"c{tau}.g{G}.kqme")
            for h in hs:
                MM(ps["psA1"][:, (h % 4) * 128:(h % 4 + 1) * 128],
                   t[f"qcb{p}"][:, h * 128:(h + 1) * 128],
                   t["skf_s"][:, h * 128:(h + 1) * 128], start=True, stop=False)
                MM(ps["psA1"][:, (h % 4) * 128:(h % 4 + 1) * 128],
                   t["kqm_s"][:, (h % 4) * 128:(h % 4 + 1) * 128],
                   t["U_b"][:, h * 128:(h + 1) * 128], start=False, stop=True)
            kb.mark(f"c{tau}.g{G}.ms", "T")
            kb.wait(e, f"c{tau}.r5e" if G == 0 else f"c{tau}.g0.ate")
            kb.wait(e, f"c{tau}.g{G}.ab")
            for h in hs:
                TR(ps["psT"][:, (h % 4) * 128 + G * 512:(h % 4 + 1) * 128 + G * 512],
                   t["ab_s"][:, (h % 4) * 128:(h % 4 + 1) * 128])
            kb.mark(f"c{tau}.g{G}.att", "T")
            if (tau, G) != (0, 0):
                pt, pg = (tau, 0) if G == 1 else (tau - 1, 1)
                kb.wait(e, f"c{pt}.g{pg}.pmte")
            kb.wait(e, f"c{tau}.g{G}.ate")
            for h in hs:
                MM(ps["psA2"][:, (h % 4) * 128:(h % 4 + 1) * 128],
                   t["um_s"][:, h * 128:(h + 1) * 128],
                   t["at_s"][:, (h % 4) * 128:(h % 4 + 1) * 128],
                   start=True, stop=True)
            kb.mark(f"c{tau}.g{G}.mp", "T")
            if (tau, G) != (0, 0):
                pt, pg = (tau, 0) if G == 1 else (tau - 1, 1)
                kb.wait(e, f"c{pt}.g{pg}.oe")
            kb.wait(e, f"c{tau}.g{G}.pmte", f"c{tau}.SvFb")
            for h in hs:
                MM(ps["psA3"][:, (h % 4) * 128:(h % 4 + 1) * 128],
                   t["at_s"][:, (h % 4) * 128:(h % 4 + 1) * 128],
                   t["SvF_b"][:, h * 128:(h + 1) * 128], start=True, stop=False)
                MM(ps["psA3"][:, (h % 4) * 128:(h % 4 + 1) * 128],
                   t["pmt_s"][:, (h % 4) * 128:(h % 4 + 1) * 128],
                   t["vt_s"][:, h * 128:(h + 1) * 128], start=False, stop=True)
            kb.mark(f"c{tau}.g{G}.mo", "T")
            kb.wait(e, f"c{tau}.E" if G == 0 else f"c{tau}.g0.st")
            for h in hs:
                MM(ps["psB"][:, (h % 4) * 128:(h % 4 + 1) * 128],
                   t["U_b"][:, h * 128:(h + 1) * 128],
                   t["kt_s"][:, h * 128:(h + 1) * 128], start=True, stop=True)
                MM(ps["psB"][:, 512 + (h % 4) * 128:512 + (h % 4 + 1) * 128],
                   t["U_b"][:, h * 128:(h + 1) * 128],
                   t["vt_s"][:, h * 128:(h + 1) * 128], start=True, stop=True)
            kb.mark(f"c{tau}.g{G}.ukuv", "T")
        kb.wait(e, f"c{tau}.g1.ate", f"c{tau}.ogb")
        for h in range(NH):
            TR(ps["psT"][:, h * 128:(h + 1) * 128], t["og_b"][:, h * 128:(h + 1) * 128])
        kb.mark(f"c{tau}.r6", "T")
        kb.mark(f"c{tau}.tlast", "T")
    kb.mark("p2.end.T", "T")


def p2_vector(kb, e):
    t, ps = kb.t, kb.ps
    V = lambda *a, **k: kb.do(e, *a, sem="V", **k)
    kb.wait(e, "p1.T", "p1.V", "p1.S", "p1.ds", "p1.dg")
    b3 = lambda buf, w=8: bass.AP(t[buf], 0, [[w, 128], [1, w], [0, 128]])
    for tau in range(NTC):
        p = tau % 2
        kb.wait(e, f"c{tau}.cols")
        V("tensor_copy", t["ccol"][:, :], ps["psC"][:, 0:16])
        kb.mark(f"c{tau}.ccol", "V")
        kb.wait(e, f"c{tau}.ccol")
        V("tensor_tensor", t["tmp8"][:, :], t["pcol"][:, :],
          bass.AP(t["ccol"], 1, [[16, 128], [2, 8]]), ALU.add)
        kb.mark(f"c{tau}.addc", "V")
        V("tensor_copy", t["pcol"][:, :], bass.AP(t["ccol"], 0, [[16, 128], [2, 8]]))
        kb.wait(e, f"c{tau}.wsq")
        V("reduce_sum", t["ssq8"][:, :],
          t["scrF"][:, :].rearrange("p (h u) -> p h u", u=128), axis=AX.X)
        kb.mark(f"c{tau}.ssq", "V")
        kb.wait(e, f"c{tau}.wr")
        V("tensor_tensor", t["rb8"][:, :], t["wr8"][:, :],
          t["beta_t"][:, tau * 8:(tau + 1) * 8], ALU.mult)
        kb.mark(f"c{tau}.rb8", "V")
        kb.wait(e, f"c{tau}.E", f"c{tau}.rb8", f"c{tau}.r1e")
        V("tensor_tensor", t["U_f"][:, :], t["wt_s"][:, :], t["E_s"][:, :],
          ALU.mult)
        V("tensor_tensor", t["U_f"][:, :].rearrange("p (h u) -> p h u", u=128),
          t["U_f"][:, :].rearrange("p (h u) -> p h u", u=128), b3("rb8"),
          ALU.mult)
        V("tensor_copy", t["U_b"][:, :], t["U_f"][:, :])
        kb.mark(f"c{tau}.Ub", "V")
        kb.wait(e, f"c{tau}.comb")
        V("tensor_tensor", t["SkF_f"][:, :].rearrange("p (h u) -> p h u", u=128),
          t["SkT"][:, :].rearrange("p (h u) -> p h u", u=128), b3("comb_s"),
          ALU.mult)
        V("tensor_copy", t["SkF_b"][:, :], t["SkF_f"][:, :])
        kb.mark(f"c{tau}.SkFb", "V")
        V("tensor_tensor", t["SvF_f"][:, :].rearrange("p (h u) -> p h u", u=128),
          t["Sv"][:, :].rearrange("p (h u) -> p h u", u=128), b3("comb_s"),
          ALU.mult)
        V("tensor_copy", t["SvF_b"][:, :], t["SvF_f"][:, :])
        kb.mark(f"c{tau}.SvFb", "V")
        for G in range(2):
            gsl = slice(G * 512, (G + 1) * 512)
            kb.wait(e, f"c{tau}.g{G}.kq")
            V("tensor_tensor", t["kqm_s"][:, :].rearrange("p (h u) -> p h u", u=128),
              ps["psA0"][:, :].rearrange("p (h u) -> p h u", u=128),
              bass.AP(t["mask_s"], 0, [[128, 128], [0, 4], [1, 128]]), ALU.mult)
            kb.mark(f"c{tau}.g{G}.kqme", "V")
            kb.wait(e, f"c{tau}.g{G}.ms")
            V("scalar_tensor_tensor", t["s_s"][:, :], ps["psA1"][:, :], SCALE,
              t["D_s"][:, gsl], ALU.mult, ALU.mult)
            kb.mark(f"c{tau}.g{G}.sev", "V")
            V("reduce_max", t["nm8"][:, G * 4:G * 4 + 4],
              t["s_s"][:, :].rearrange("p (h u) -> p h u", u=128), axis=AX.X,
              negate=True)
            kb.mark(f"c{tau}.g{G}.nm", "V")
            kb.wait(e, f"c{tau}.g{G}.nm")
            V("tensor_tensor", t["sm_s"][:, :].rearrange("p (h u) -> p h u", u=128),
              t["s_s"][:, :].rearrange("p (h u) -> p h u", u=128),
              bass.AP(t["nm8"], G * 4, [[8, 128], [1, 4], [0, 128]]), ALU.add)
            kb.mark(f"c{tau}.g{G}.sm", "V")
            kb.wait(e, f"c{tau}.g{G}.es")
            V("reduce_sum", t["ssm8"][:, G * 4:G * 4 + 4],
              t["es_s"][:, :].rearrange("p (h u) -> p h u", u=128), axis=AX.X)
            kb.mark(f"c{tau}.g{G}.ssum", "V")
            kb.wait(e, f"c{tau}.g{G}.rcp")
            V("tensor_tensor", t["af_s"][:, :].rearrange("p (h u) -> p h u", u=128),
              t["es_s"][:, :].rearrange("p (h u) -> p h u", u=128),
              bass.AP(t["rcp8"], G * 4, [[8, 128], [1, 4], [0, 128]]), ALU.mult)
            V("tensor_tensor", t["af_s"][:, :], t["af_s"][:, :], t["D_s"][:, gsl],
              ALU.mult)
            V("tensor_copy", t["ab_s"][:, :], t["af_s"][:, :])
            kb.mark(f"c{tau}.g{G}.ab", "V")
            kb.wait(e, f"c{tau}.g{G}.mp")
            V("tensor_tensor", t["pmt_s"][:, :].rearrange("p (h u) -> p h u", u=128),
              ps["psA2"][:, :].rearrange("p (h u) -> p h u", u=128),
              bass.AP(t["mask_s"], 0, [[128, 128], [0, 4], [1, 128]]), ALU.mult)
            kb.mark(f"c{tau}.g{G}.pmte", "V")
            kb.wait(e, f"c{tau}.g{G}.ukuv")
            V("tensor_tensor", t["SkT"][:, gsl], t["SkF_f"][:, gsl],
              ps["psB"][:, 0:512], ALU.add)
            V("tensor_tensor", t["Sv"][:, gsl], t["SvF_f"][:, gsl],
              ps["psB"][:, 512:1024], ALU.add)
            kb.mark(f"c{tau}.g{G}.st", "V")
        kb.wait(e, f"c{tau}.osq")
        V("reduce_sum", t["orms8"][:, :],
          t["scrF"][:, :].rearrange("p (h u) -> p h u", u=128), axis=AX.X)
        kb.mark(f"c{tau}.orms", "V")
        kb.wait(e, f"c{tau}.ro")
        V("tensor_tensor", t["o_s"][:, :].rearrange("p (h u) -> p h u", u=128),
          t["o_s"][:, :].rearrange("p (h u) -> p h u", u=128), b3("ro8"),
          ALU.mult)
        V("tensor_tensor", t["o_s"][:, :], t["o_s"][:, :], t[f"sgc{p}"][:, :],
          ALU.mult)
        V("tensor_copy", t["og_b"][:, :], t["o_s"][:, :])
        kb.mark(f"c{tau}.ogb", "V")
        kb.mark(f"c{tau}.vlast", "V")
    kb.mark("p2.end.V", "V")


def p2_scalar(kb, e):
    t, ps = kb.t, kb.ps
    S = lambda *a, **k: kb.do(e, "activation", *a, sem="S", **k)
    kb.wait(e, "p1.T", "p1.V", "p1.S", "p1.ds", "p1.dg")
    load_act_set(kb, e, 6)
    kb.do(e, "activation", t["wrm"][0:1, 0:8], t["wrm"][0:1, 0:8], AF.Exp)
    for tau in range(NTC):
        kb.wait(e, f"c{tau}.cs")
        S(t["D_s"][:, :], ps["psB"][:, :], AF.Exp)
        S(t["E_s"][:, :], ps["psB"][:, :], AF.Exp, scale=-1.0)
        kb.mark(f"c{tau}.E", "S")
        kb.wait(e, f"c{tau}.addc")
        S(t["comb_s"][:, :], t["tmp8"][:, :], AF.Exp)
        kb.mark(f"c{tau}.comb", "S")
        kb.wait(e, f"c{tau}.r1")
        S(t["wt_s"][:, :], ps["psT"][:, :], AF.Copy)
        kb.mark(f"c{tau}.r1e", "S")
        S(t["scrF"][:, :], t["wt_s"][:, :], AF.Square)
        kb.mark(f"c{tau}.wsq", "S")
        kb.wait(e, f"c{tau}.ssq")
        S(t["wr8"][:, :], t["ssq8"][:, :], AF.Ln, bias=t["eps6"][:, :])
        S(t["wr8"][:, :], t["wr8"][:, :], AF.Exp, scale=-0.5)
        kb.mark(f"c{tau}.wr", "S")
        for rr, dst in [("r2", "um_s"), ("r3", "kt_s"), ("r4", "vt_s"),
                        ("r5", "skf_s")]:
            kb.wait(e, f"c{tau}.{rr}")
            S(t[dst][:, :], ps["psT"][:, :], AF.Copy)
            kb.mark(f"c{tau}.{rr}e", "S")
        for G in range(2):
            gsl = slice(G * 512, (G + 1) * 512)
            kb.wait(e, f"c{tau}.g{G}.sm")
            S(t["es_s"][:, :], t["sm_s"][:, :], AF.Exp)
            kb.mark(f"c{tau}.g{G}.es", "S")
            kb.wait(e, f"c{tau}.g{G}.ssum")
            S(t["rcp8"][:, G * 4:G * 4 + 4], t["ssm8"][:, G * 4:G * 4 + 4],
              AF.Ln)
            S(t["rcp8"][:, G * 4:G * 4 + 4], t["rcp8"][:, G * 4:G * 4 + 4],
              AF.Exp, scale=-1.0)
            kb.mark(f"c{tau}.g{G}.rcp", "S")
            kb.wait(e, f"c{tau}.g{G}.att")
            S(t["at_s"][:, :], ps["psT"][:, gsl], AF.Copy)
            kb.mark(f"c{tau}.g{G}.ate", "S")
            if G == 0 and tau >= 1:
                kb.wait(e, f"c{tau - 1}.ogb")
            kb.wait(e, f"c{tau}.g{G}.mo")
            S(t["o_s"][:, gsl], ps["psA3"][:, :], AF.Copy)
            kb.mark(f"c{tau}.g{G}.oe", "S")
        kb.wait(e, f"c{tau}.ssq")
        S(t["scrF"][:, :], t["o_s"][:, :], AF.Square)
        kb.mark(f"c{tau}.osq", "S")
        kb.wait(e, f"c{tau}.orms")
        S(t["ro8"][:, :], t["orms8"][:, :], AF.Ln, scale=1.0 / 128.0,
          bias=t["eps5"][:, :])
        S(t["ro8"][:, :], t["ro8"][:, :], AF.Exp, scale=-0.5)
        kb.mark(f"c{tau}.ro", "S")
        kb.wait(e, f"c{tau}.r6")
        if tau >= 2:
            kb.wait(e, f"c{tau - 2}.ost")
        S(t[f"ot{tau % 2}"][:, :], ps["psT"][:, :], AF.Copy)
        kb.mark(f"c{tau}.ote", "S")
    kb.mark("p2.end.S", "S")



P2END = ("p2.end.sync", "p2.end.T", "p2.end.V", "p2.end.S")


def p3_gpsimd(kb, g):
    t = kb.t
    D = lambda *a, **k: kb.do(g, "dma_start", *a, sem="dg", n=16, **k)
    kb.wait(g, *P2END)
    for hc in range(16):
        D(t["wo_s"][:, hc * 2048:(hc + 1) * 2048].rearrange(
            "p (a b) -> p a b", b=128),
          bass.AP(t["Wo"], hc * 128 * HID, [[HID, 128], [128, NFC], [1, 128]]))
        kb.mark(f"p3.wo.{hc}", "dg")
    D(t["meta_s"][:, :], t["meta"].ap()[:, :])
    kb.mark("p3.wosent", "dg")
    for tt in range(4):
        if tt >= 1:
            kb.wait(g, f"p3.radd.{tt - 1}")
        D(t["hres0"][:, :].rearrange("p (a b) -> p a b", b=512),
          bass.AP(t["hTres"], tt * 512, [[TK, 128], [128 * TK, NFC], [1, 512]]))
        kb.mark(f"p3.hres.{tt}", "dg")
        D(t["meta_s"][:, :], t["meta"].ap()[:, :])
        kb.mark(f"p3.hsentt.{tt}", "dg")
    D(t["meta_s"][:, :], t["meta"].ap()[:, :])
    kb.mark("p3.hsent", "dg")


def p3_sync(kb, s):
    t = kb.t
    D = lambda *a, **k: kb.do(s, "dma_start", *a, sem="ds", n=16, **k)
    kb.wait(s, *P2END)
    kb.wait(s, "ccp")
    for tt in range(4):
        if tt < 2:
            D(t[f"oF{tt % 2}"][:, :].rearrange("p (a b) -> p a b", b=512),
              bass.AP(t["ccP"], tt * 512, [[2048, 128], [128 * 2048, 16], [1, 512]]))
            kb.mark(f"p3.oF.{tt}", "ds")
    for tt in range(4):
        kb.wait(s, f"p3.x1.{tt}")
        D(bass.AP(t["sp_x1"], tt * 512, [[TK, 128], [128 * TK, NFC], [1, 512]]),
          t["x1t"][:, :].rearrange("p (a b) -> p a b", b=512))
        kb.mark(f"p3.x1sp.{tt}", "ds")
        if tt + 2 < 4:
            kb.wait(s, f"p3.mm.{tt}")
            D(t[f"oF{tt % 2}"][:, :].rearrange("p (a b) -> p a b", b=512),
              bass.AP(t["ccP"], (tt + 2) * 512,
                      [[2048, 128], [128 * 2048, 16], [1, 512]]))
            kb.mark(f"p3.oF.{tt + 2}", "ds")
        if tt == 1:
            D(t["meta_s"][:, :], t["meta"].ap()[:, :])
            kb.mark("p3.osent", "ds")
    kb.mark("p3.end.sync", "ds")


def p3_tensor(kb, e):
    t, ps = kb.t, kb.ps
    MM = lambda *a, **k: kb.do(e, "matmul", *a, sem="T", **k)
    kb.wait(e, *P2END)
    kb.wait(e, "p3.wosent", "v.nwfold")
    for tt in range(4):
        oF = t[f"oF{tt % 2}"]
        kb.wait(e, f"p3.oF.{tt + 1}" if tt < 3 else "p3.osent")
        for fc in range(NFC):
            pb = ps[f"psA{fc % 2}"]
            if (tt, fc) > (0, 1):
                pfc = fc - 2 if fc >= 2 else NFC - 2 + fc
                ptt = tt if fc >= 2 else tt - 1
                kb.wait(e, f"p3.radd.{ptt}.{pfc}")
            if P3LIM >= 1:
                for hc in range(16):
                    MM(pb[:, :], t["wo_s"][:, hc * 2048 + fc * 128:hc * 2048 + fc * 128 + 128],
                       oF[:, hc * 512:(hc + 1) * 512],
                       start=(hc == 0), stop=(hc == 15))
            kb.mark(f"p3.mmf.{tt}.{fc}", "T")
        kb.mark(f"p3.mm.{tt}", "T")
        for fc in range(NFC):
            kb.wait(e, f"p3.rb.{tt}.{fc}", f"p3.sqb.{tt}.{fc}")
            if fc == 0 and tt > 0:
                kb.wait(e, f"p3.x1.{tt - 1}")
            if P3LIM >= 2:
                MM(ps["psC"][0:1, :], t["ones_c"][:, :],
                   t["rb"][:, fc * 512:(fc + 1) * 512],
                   start=(fc == 0), stop=(fc == NFC - 1))
                MM(ps["psA2"][0:1, :], t["ones_c"][:, :], t["sqb"][:, :],
                   start=(fc == 0), stop=(fc == NFC - 1))
            kb.mark(f"p3.smm.{tt}.{fc}", "T")
        kb.mark(f"p3.stat.{tt}", "T")
        kb.wait(e, f"p3.rstd.{tt}")
        if P3LIM >= 3:
            MM(ps["psB"][:, 0:512], t["ones_r32"][:, :], t["musb"][0:1, :],
               start=True, stop=True)
            MM(ps["psB"][:, 512:1024], t["ones_r32"][:, :], t["rstd"][0:1, :],
               start=True, stop=True)
        kb.mark(f"p3.bc.{tt}", "T")
    kb.mark("p3.end.T", "T")


def p3_vector(kb, e):
    t, ps = kb.t, kb.ps
    V = lambda *a, **k: kb.do(e, *a, sem="V", **k)
    kb.wait(e, *P2END)
    for hc in range(16):
        kb.wait(e, f"p3.wo.{hc + 1}" if hc < 15 else "p3.wosent")
        V("tensor_scalar_mul", t["r_f"][:, 0:2048],
          t["wo_s"][:, hc * 2048:(hc + 1) * 2048], t["nw_s"][:, :])
        kb.do(e, "engine_nop", sem="V")
        kb.mark(f"v.nwtmp.{hc}", "V")
        kb.wait(e, f"v.nwtmp.{hc}")
        V("tensor_copy", t["wo_s"][:, hc * 2048:(hc + 1) * 2048],
          t["r_f"][:, 0:2048])
    kb.mark("v.nwfold", "V")
    for tt in range(4):
        hres = t["hres0"]
        kb.wait(e, f"p3.hsentt.{tt}")
        if tt > 0:
            kb.wait(e, f"p3.x1sp.{tt - 1}")
        for fc in range(NFC):
            pb = ps[f"psA{fc % 2}"]
            kb.wait(e, f"p3.mmf.{tt}.{fc}")
            V("tensor_tensor", t["r_f"][:, fc * 512:(fc + 1) * 512],
              pb[:, :], hres[:, fc * 512:(fc + 1) * 512], ALU.add)
            kb.mark(f"p3.radd.{tt}.{fc}", "V")
            V("tensor_copy", t["rb"][:, fc * 512:(fc + 1) * 512],
              t["r_f"][:, fc * 512:(fc + 1) * 512])
            kb.mark(f"p3.rb.{tt}.{fc}", "V")
        kb.mark(f"p3.radd.{tt}", "V")
        kb.wait(e, f"p3.musq.{tt}")
        V("tensor_tensor", t["var_"][0:1, :], t["sqsb"][0:1, :],
          t["var_"][0:1, :], ALU.subtract)
        kb.mark(f"p3.var.{tt}", "V")
        kb.wait(e, f"p3.bc.{tt}")
        if tt > 0:
            kb.wait(e, f"p3.x1sp.{tt - 1}")
        for fc in range(NFC):
            V("tensor_tensor", t["r_f"][:, fc * 512:(fc + 1) * 512],
              t["r_f"][:, fc * 512:(fc + 1) * 512], ps["psB"][:, 0:512],
              ALU.subtract)
            V("tensor_tensor", t["x1t"][:, fc * 512:(fc + 1) * 512],
              t["r_f"][:, fc * 512:(fc + 1) * 512], ps["psB"][:, 512:1024],
              ALU.mult)
        kb.mark(f"p3.x1.{tt}", "V")
    kb.mark("p3.end.V", "V")


def p3_scalar(kb, e):
    t, ps = kb.t, kb.ps
    kb.wait(e, *P2END)
    for tt in range(4):
        for fc in range(NFC):
            kb.wait(e, f"p3.rb.{tt}.{fc}")
            if fc > 0:
                kb.wait(e, f"p3.smm.{tt}.{fc - 1}")
            kb.do(e, "activation", t["sqb"][:, :],
                  t["r_f"][:, fc * 512:(fc + 1) * 512], AF.Square, sem="S")
            kb.mark(f"p3.sqb.{tt}.{fc}", "S")
        kb.wait(e, f"p3.stat.{tt}")
        kb.do(e, "activation", t["musb"][0:1, :], ps["psC"][0:1, :],
              AF.Copy, scale=1.0 / HID)
        kb.do(e, "activation", t["sqsb"][0:1, :], ps["psA2"][0:1, :],
              AF.Copy, scale=1.0 / HID, sem="S")
        kb.mark(f"p3.mus.{tt}", "S")
        kb.wait(e, f"p3.mus.{tt}")
        kb.do(e, "activation", t["var_"][0:1, :], t["musb"][0:1, :],
              AF.Square, sem="S")
        kb.mark(f"p3.musq.{tt}", "S")
        kb.wait(e, f"p3.var.{tt}")
        kb.do(e, "activation", t["rstd"][0:1, :], t["var_"][0:1, :], AF.Ln,
              bias=t["eps5"][0:1, :])
        kb.do(e, "activation", t["rstd"][0:1, :], t["rstd"][0:1, :], AF.Exp,
              scale=-0.5, sem="S")
        kb.mark(f"p3.rstd.{tt}", "S")
    kb.mark("p3.end.S", "S")


P3END = ("p3.end.sync", "p3.end.T", "p3.end.V", "p3.end.S")


def p4_gpsimd(kb, g):
    t = kb.t
    D = lambda *a, **k: kb.do(g, "dma_start", *a, sem="dg", n=16, **k)
    kb.wait(g, *P3END)
    for fc in range(NFC):
        D(t["x1a"][:, fc * TK:(fc + 1) * TK],
          bass.AP(t["sp_x1"], fc * 128 * TK, [[TK, 128], [1, TK]]))
    D(t["meta_s"][:, :], t["meta"].ap()[:, :])
    kb.mark("p4.x1a", "dg")
    D(t["sg8"][:, :], t["s_gate"].ap()[:, :])
    D(t["su8"][:, :], t["s_up"].ap()[:, :])
    D(t["sd8"][:, :], t["s_down"].ap()[:, :])
    D(t["meta_s"][:, :], t["meta"].ap()[:, :])
    kb.mark("p4.scales", "dg")
    for i in range(NI):
        if i >= 2:
            kb.wait(g, f"p4.gu.{i - 2}")
        D(t["wg0" if i % 2 == 0 else "wg1_"][:, :].rearrange(
            "p (a b) -> p a b", b=128),
          bass.AP(t["wg8"], i * NFC * 128 * 128, [[128, 128], [128 * 128, NFC], [1, 128]]))
        D(t["wu0" if i % 2 == 0 else "wu1_"][:, :].rearrange(
            "p (a b) -> p a b", b=128),
          bass.AP(t["wu8"], i * NFC * 128 * 128, [[128, 128], [128 * 128, NFC], [1, 128]]))
        D(t["meta_s"][:, :], t["meta"].ap()[:, :])
        kb.mark(f"p4.ws.{i}", "dg")
    kb.mark("p4.wsent", "dg")
    for tb in range(4):
        for fc in range(NFC):
            wdi = tb * NFC + fc
            if wdi >= 1:
                kb.wait(g, f"p4.dmmA.{wdi - 1}")
            D(t["wd_s"][:, 0:32 * 128].rearrange("p (a b) -> p a b", b=128),
              bass.AP(t["wd8"], fc * NI * 128 * 128,
                      [[128, 128], [128 * 128, 32], [1, 128]]))
            D(t["meta_s"][:, :], t["meta"].ap()[:, :])
            kb.mark(f"p4.wdsA.{wdi}", "dg")
            if wdi >= 1:
                kb.wait(g, f"p4.dmm.{wdi - 1}")
            D(t["wd_s"][:, 32 * 128:].rearrange("p (a b) -> p a b", b=128),
              bass.AP(t["wd8"], fc * NI * 128 * 128 + 32 * 128 * 128,
                      [[128, 128], [128 * 128, 32], [1, 128]]))
            D(t["meta_s"][:, :], t["meta"].ap()[:, :])
            kb.mark(f"p4.wdsB.{wdi}", "dg")
    kb.mark("p4.dsent", "dg")


def p4_sync(kb, s):
    t = kb.t
    D = lambda *a, **k: kb.do(s, "dma_start", *a, sem="ds", n=16, **k)
    kb.wait(s, *P3END)
    for i in range(NI):
        kb.wait(s, f"p4.fev.{i}")
        D(bass.AP(t["fuse"], i * 128 * TK, [[TK, 128], [1, TK]]),
          t[f"fstg{i % 2}"][:, :])
        kb.mark(f"p4.fuse.{i}", "ds")
    SENT = lambda: D(t["meta_s"][:, :], t["meta"].ap()[:, :])
    for tb in range(4):
        kb.wait(s, f"p4.fuse.{NI - 1}")
        if tb >= 1:
            kb.wait(s, f"p4.dmmA.{tb * NFC - 1}")
        D(t["fusS"][:, 0:32 * 512].rearrange("p (a b) -> p a b", b=512),
          bass.AP(t["fuse"], tb * 512, [[TK, 128], [128 * TK, 32], [1, 512]]))
        SENT()
        kb.mark(f"p4.fusA.{tb}", "ds")
        if tb >= 1:
            kb.wait(s, f"p4.dmm.{tb * NFC - 1}")
        D(t["fusS"][:, 32 * 512:].rearrange("p (a b) -> p a b", b=512),
          bass.AP(t["fuse"], 32 * 128 * TK + tb * 512,
                  [[TK, 128], [128 * TK, 32], [1, 512]]))
        SENT()
        kb.mark(f"p4.fusB.{tb}", "ds")
        if tb >= 2:
            kb.wait(s, f"p4.out.{tb - 2}")
        D(t[f"x1r{tb % 2}"][:, :].rearrange("p (a b) -> p a b", b=512),
          bass.AP(t["sp_x1"], tb * 512, [[TK, 128], [128 * TK, NFC], [1, 512]]))
        SENT()
        kb.mark(f"p4.x1r.{tb}", "ds")
        if tb >= 1:
            for fc in range(NFC):
                kb.wait(s, f"p4.x2.{tb - 1}.{fc}")
                D(bass.AP(t["outT"], (tb - 1) * 512 + fc * 128 * TK,
                          [[TK, 128], [1, 512]]),
                  t["r_f"][:, fc * 512:(fc + 1) * 512])
                kb.mark(f"p4.out.{tb - 1}.{fc}", "ds")
            kb.mark(f"p4.out.{tb - 1}", "ds")
    for fc in range(NFC):
        kb.wait(s, f"p4.x2.3.{fc}")
        D(bass.AP(t["outT"], 3 * 512 + fc * 128 * TK, [[TK, 128], [1, 512]]),
          t["r_f"][:, fc * 512:(fc + 1) * 512])
        kb.mark(f"p4.out.3.{fc}", "ds")
    kb.mark("p4.out.3", "ds")
    kb.mark("p4.end.sync", "ds")


def p4_tensor(kb, e):
    t, ps = kb.t, kb.ps
    MM = lambda *a, **k: kb.do(e, "matmul", *a, sem="T", **k)
    kb.wait(e, *P3END)
    kb.wait(e, "p4.x1a", "p4.scales")
    for i in range(NI):
        kb.wait(e, f"p4.ws.{i}")
        for tb in range(4):
            pg = ps[f"psA{tb % 2}"]
            pu = ps[f"psA{2 + tb % 2}"]
            if (i, tb) > (0, 1):
                pi, ptb = (i, tb - 2) if tb >= 2 else (i - 1, tb + 2)
                kb.wait(e, f"p4.fev.{pi}.{ptb}")
            wgb = t["wg0" if i % 2 == 0 else "wg1_"]
            wub = t["wu0" if i % 2 == 0 else "wu1_"]
            for fc in range(NFC):
                MM(pg[:, :], wgb[:, fc * 128:(fc + 1) * 128],
                   t["x1a"][:, fc * TK + tb * 512: fc * TK + tb * 512 + 512],
                   start=(fc == 0), stop=(fc == NFC - 1))
                MM(pu[:, :], wub[:, fc * 128:(fc + 1) * 128],
                   t["x1a"][:, fc * TK + tb * 512: fc * TK + tb * 512 + 512],
                   start=(fc == 0), stop=(fc == NFC - 1))
            kb.mark(f"p4.gumm.{i}.{tb}", "T")
        kb.mark(f"p4.gu.{i}", "T")
    for tb in range(4):
        for fc in range(NFC):
            wdi = tb * NFC + fc
            kb.wait(e, f"p4.wdsA.{wdi}")
            if fc == 0:
                kb.wait(e, f"p4.fusA.{tb}")
            pb = ps["psA3" if wdi % 3 == 2 else f"psA{wdi % 3}"]
            if wdi >= 3:
                kb.wait(e, f"p4.dsc.{wdi - 3}")
            for i in range(32):
                MM(pb[:, :], t["wd_s"][:, i * 128:(i + 1) * 128],
                   t["fusS"][:, i * 512:(i + 1) * 512],
                   start=(i == 0), stop=False)
            kb.mark(f"p4.dmmA.{wdi}", "T")
            kb.wait(e, f"p4.wdsB.{wdi}")
            if fc == 0:
                kb.wait(e, f"p4.fusB.{tb}")
            for i in range(32, NI):
                MM(pb[:, :], t["wd_s"][:, i * 128:(i + 1) * 128],
                   t["fusS"][:, i * 512:(i + 1) * 512],
                   start=False, stop=(i == NI - 1))
            kb.mark(f"p4.dmm.{wdi}", "T")
        for fc in range(NFC):
            kb.wait(e, f"p4.rb2.{tb}.{fc}", f"p4.sqb2.{tb}.{fc}")
            if fc == 0 and tb > 0:
                kb.wait(e, f"p4.x2.{tb - 1}")
            MM(ps["psC"][0:1, :], t["ones_c"][:, :],
               t["rb"][:, fc * 512:(fc + 1) * 512],
               start=(fc == 0), stop=(fc == NFC - 1))
            MM(ps["psA2"][0:1, :], t["ones_c"][:, :], t["sqb"][:, :],
               start=(fc == 0), stop=(fc == NFC - 1))
            kb.mark(f"p4.smm.{tb}.{fc}", "T")
        kb.mark(f"p4.stat.{tb}", "T")
        kb.wait(e, f"p4.rstd.{tb}")
        MM(ps["psB"][:, 0:512], t["ones_r32"][:, :], t["musb"][0:1, :],
           start=True, stop=True)
        MM(ps["psB"][:, 512:1024], t["ones_r32"][:, :], t["rstd"][0:1, :],
           start=True, stop=True)
        kb.mark(f"p4.bc.{tb}", "T")
    kb.mark("p4.end.T", "T")


def p4_vector(kb, e):
    t, ps = kb.t, kb.ps
    V = lambda *a, **k: kb.do(e, *a, sem="V", **k)
    kb.wait(e, *P3END)
    for i in range(NI):
        for tb in range(4):
            pu = ps[f"psA{2 + tb % 2}"]
            kb.wait(e, f"p4.gumm.{i}.{tb}", f"p4.sil.{i}.{tb}")
            if i >= 2:
                kb.wait(e, f"p4.fuse.{i - 2}")
            V("scalar_tensor_tensor",
              t[f"fstg{i % 2}"][:, tb * 512:(tb + 1) * 512], pu[:, :],
              t["su8"][:, i:i + 1], t["gsil"][:, tb * 512:(tb + 1) * 512],
              ALU.mult, ALU.mult)
            kb.mark(f"p4.fev.{i}.{tb}", "V")
        kb.mark(f"p4.fev.{i}", "V")
    for tb in range(4):
        x1r = t[f"x1r{tb % 2}"]
        for fc in range(NFC):
            wdi = tb * NFC + fc
            pb = ps[f"psA{fc % 2}"]
            kb.wait(e, f"p4.dsc.{wdi}")
            V("tensor_tensor", t["r_f"][:, fc * 512:(fc + 1) * 512],
              t["r_f"][:, fc * 512:(fc + 1) * 512],
              x1r[:, fc * 512:(fc + 1) * 512], ALU.add)
            kb.mark(f"p4.dev.{wdi}", "V")
            V("tensor_copy", t["rb"][:, fc * 512:(fc + 1) * 512],
              t["r_f"][:, fc * 512:(fc + 1) * 512])
            kb.mark(f"p4.rb2.{tb}.{fc}", "V")
        kb.wait(e, f"p4.musq.{tb}")
        V("tensor_tensor", t["var_"][0:1, :], t["sqsb"][0:1, :],
          t["var_"][0:1, :], ALU.subtract)
        kb.mark(f"p4.var.{tb}", "V")
        kb.wait(e, f"p4.bc.{tb}")
        if tb > 0:
            kb.wait(e, f"p4.out.{tb - 1}")
        for fc in range(NFC):
            V("tensor_tensor", t["r_f"][:, fc * 512:(fc + 1) * 512],
              t["r_f"][:, fc * 512:(fc + 1) * 512], ps["psB"][:, 0:512],
              ALU.subtract)
            V("tensor_tensor", t["r_f"][:, fc * 512:(fc + 1) * 512],
              t["r_f"][:, fc * 512:(fc + 1) * 512], ps["psB"][:, 512:1024],
              ALU.mult)
            kb.mark(f"p4.x2.{tb}.{fc}", "V")
        kb.mark(f"p4.x2.{tb}", "V")
    kb.mark("p4.end.V", "V")


def p4_scalar(kb, e):
    t, ps = kb.t, kb.ps
    kb.wait(e, *P3END)
    load_act_set(kb, e, 18)
    kb.do(e, "activation", t["wrm"][0:1, 0:8], t["wrm"][0:1, 0:8], AF.Silu)
    for i in range(NI):
        for tb in range(4):
            pg = ps[f"psA{tb % 2}"]
            kb.wait(e, f"p4.gumm.{i}.{tb}")
            if i >= 1:
                kb.wait(e, f"p4.fev.{i - 1}.{tb}")
            kb.do(e, "activation", t["gsil"][:, tb * 512:(tb + 1) * 512],
                  pg[:, :], AF.Silu, scale=t["sg8"][:, i:i + 1], sem="S")
            kb.mark(f"p4.sil.{i}.{tb}", "S")
    load_act_set(kb, e, 6)
    kb.do(e, "activation", t["wrm"][0:1, 0:8], t["wrm"][0:1, 0:8], AF.Exp)
    for tb in range(4):
        for fc in range(NFC):
            wdi = tb * NFC + fc
            pb = ps["psA3" if wdi % 3 == 2 else f"psA{wdi % 3}"]
            kb.wait(e, f"p4.dmm.{wdi}")
            if tb > 0:
                kb.wait(e, f"p4.out.{tb - 1}.{fc}")
            kb.do(e, "activation", t["r_f"][:, fc * 512:(fc + 1) * 512],
                  pb[:, :], AF.Copy, scale=t["sd8"][:, fc:fc + 1], sem="S")
            kb.mark(f"p4.dsc.{wdi}", "S")
        for fc in range(NFC):
            kb.wait(e, f"p4.rb2.{tb}.{fc}")
            if fc > 0:
                kb.wait(e, f"p4.smm.{tb}.{fc - 1}")
            kb.do(e, "activation", t["sqb"][:, :],
                  t["r_f"][:, fc * 512:(fc + 1) * 512], AF.Square, sem="S")
            kb.mark(f"p4.sqb2.{tb}.{fc}", "S")
        kb.wait(e, f"p4.stat.{tb}")
        kb.do(e, "activation", t["musb"][0:1, :], ps["psC"][0:1, :], AF.Copy,
              scale=1.0 / HID)
        kb.do(e, "activation", t["sqsb"][0:1, :], ps["psA2"][0:1, :], AF.Copy,
              scale=1.0 / HID, sem="S")
        kb.mark(f"p4.mus.{tb}", "S")
        kb.wait(e, f"p4.mus.{tb}")
        kb.do(e, "activation", t["var_"][0:1, :], t["musb"][0:1, :],
              AF.Square, sem="S")
        kb.mark(f"p4.musq.{tb}", "S")
        kb.wait(e, f"p4.var.{tb}")
        kb.do(e, "activation", t["rstd"][0:1, :], t["var_"][0:1, :], AF.Ln,
              bias=t["eps5"][0:1, :])
        kb.do(e, "activation", t["rstd"][0:1, :], t["rstd"][0:1, :], AF.Exp,
              scale=-0.5, sem="S")
        kb.mark(f"p4.rstd.{tb}", "S")
    kb.mark("p4.end.S", "S")


def emit_gpsimd(kb, g):
    if "P1" in kb.phases:
        p1_gpsimd(kb, g)
    if "P2" in kb.phases:
        p2_gpsimd(kb, g)
    if "P3" in kb.phases:
        p3_gpsimd(kb, g)
    if "P4" in kb.phases:
        p4_gpsimd(kb, g)
    kb.wait(g, "end.sync")
    if "P2" in kb.phases:
        kb.wait(g, "c31.ost", "ag.b", "ccp")
    for tap in kb.taps:
        th, to = kb.t[tap], kb.t["tap_" + tap]
        kb.do(g, "dma_start", to.ap().opt(), th.ap().opt(), sem="dg", n=16)
    kb.mark("g.taps", "dg")
    kb.wait(g, "g.taps")


def emit_sync(kb, s):
    if "P1" in kb.phases:
        p1_sync(kb, s)
    if "P2" in kb.phases:
        p2_sync(kb, s)
    if "P3" in kb.phases:
        p3_sync(kb, s)
    if "P4" in kb.phases:
        p4_sync(kb, s)
    kb.mark("end.sync", "ds")


def emit_tensor(kb, e):
    if "P1" in kb.phases:
        p1_tensor(kb, e)
    if "P2" in kb.phases:
        p2_tensor(kb, e)
    if "P3" in kb.phases:
        p3_tensor(kb, e)
    if "P4" in kb.phases:
        p4_tensor(kb, e)


def emit_vector(kb, e):
    if "P1" in kb.phases:
        p1_vector(kb, e)
    if "P2" in kb.phases:
        p2_vector(kb, e)
    if "P3" in kb.phases:
        p3_vector(kb, e)
    if "P4" in kb.phases:
        p4_vector(kb, e)


def emit_scalar(kb, e):
    if "P1" in kb.phases:
        p1_scalar(kb, e)
    if "P2" in kb.phases:
        p2_scalar(kb, e)
    if "P3" in kb.phases:
        p3_scalar(kb, e)
    if "P4" in kb.phases:
        p4_scalar(kb, e)


_CACHE = {}


def _consts():
    ident = np.eye(128, dtype=np.float32).astype(BF)
    s = np.arange(128)[:, None]
    tt = np.arange(128)[None, :]
    mcs = ((s <= 63).astype(np.float32) - (s <= tt).astype(np.float32)).astype(BF)
    mcols = np.stack([-(np.arange(128) >= 64).astype(np.float32),
                      -(np.arange(128) <= 63).astype(np.float32)], 1).astype(BF)
    mask = (s <= tt).astype(np.float32)
    return ident, mcs, mcols, mask


def make_in_maps(ip):
    h = ip["hidden_states"].astype(np.float32, copy=False)
    ident, mcs, mcols, mask = _consts()
    wg8 = np.ascontiguousarray(
        ip["w_gate"].reshape(NFC, 128, NI, 128).transpose(2, 0, 1, 3))
    wu8 = np.ascontiguousarray(
        ip["w_up"].reshape(NFC, 128, NI, 128).transpose(2, 0, 1, 3))
    wd8 = np.ascontiguousarray(
        ip["w_down"].reshape(NI, 128, NFC, 128).transpose(2, 0, 1, 3))
    Wo = ip["Wo"].reshape(16, 128, HID)
    in_maps = []
    for c in range(8):
        b, hh = c // 2, c % 2
        cs = slice(hh * CH, (hh + 1) * CH)
        cw = np.zeros((128, 128), np.float32)
        for wi, cname in enumerate(["cq", "ck", "cv", "cv"]):
            cwh = ip[cname][cs].reshape(8, 128, 4)
            for j in range(8):
                cw[:, wi * 32 + j * 4: wi * 32 + j * 4 + 4] = cwh[j]
        meta = np.zeros((1, 64), np.int32)
        meta[0, 0] = hh * TK
        meta[0, 1] = hh * (2048 * 2048)
        hTb = np.ascontiguousarray(h[b].T)
        m = {
            "hT": hTb,
            "hTres": np.ascontiguousarray(hTb[:, hh * TK:(hh + 1) * TK]),
            "Wq": np.ascontiguousarray(ip["Wq"][:, cs]),
            "Wk": np.ascontiguousarray(ip["Wk"][:, cs]),
            "Wv": np.ascontiguousarray(ip["Wv"][:, cs]),
            "Ww": np.ascontiguousarray(ip["Ww"][:, cs]),
            "cw": cw,
            "Wf1": np.ascontiguousarray(ip["Wf1"]),
            "Wf2": np.ascontiguousarray(ip["Wf2"][:, cs]),
            "Wb": np.ascontiguousarray(ip["Wb"][:, hh * NH:(hh + 1) * NH]),
            "Wg1": np.ascontiguousarray(ip["Wg1"]),
            "Wg2": np.ascontiguousarray(ip["Wg2"][:, cs]),
            "bg2": np.ascontiguousarray(ip["bg2"][None, cs]),
            "norm_w": np.ascontiguousarray(ip["norm_w"][:, None]),
            "Wo": np.ascontiguousarray(Wo),
            "wg8": wg8, "wu8": wu8, "wd8": wd8,
            "s_gate": np.ascontiguousarray(ip["s_gate"].reshape(NI, 128).T),
            "s_up": np.ascontiguousarray(ip["s_up"].reshape(NI, 128).T),
            "s_down": np.ascontiguousarray(ip["s_down"].reshape(NFC, 128).T),
            "ident": ident, "mcs": mcs, "mcols": mcols, "mask": mask,
            "meta": meta,
        }
        in_maps.append(m)
    return in_maps


def kernel(**inputs):
    ip = {k.replace("L0_attn_", "").replace("L0_ffn_", ""): np.asarray(v)
          for k, v in inputs.items()}
    key = (tuple(TAPS), tuple(PHASES))
    if key not in _CACHE:
        _CACHE[key] = build_nc(taps=TAPS, phases=PHASES)
    nc = _CACHE[key]
    in_maps = make_in_maps(ip)
    res = run_bass_kernel_spmd(nc, in_maps, core_ids=list(range(8)))
    out = np.zeros((B, T, HID), np.float32)
    for c in range(8):
        b, hh = c // 2, c % 2
        out[b, hh * TK:(hh + 1) * TK, :] = res.results[c]["outT"].T
    return out


# version 06
# speedup vs baseline: 1.0591x; 1.0591x over previous
import numpy as np
import ml_dtypes
import concourse.bass as bass
import concourse.mybir as mybir
from concourse.bass_utils import run_bass_kernel_spmd

F32 = mybir.dt.float32
BF16 = mybir.dt.bfloat16
I8 = mybir.dt.int8
I32 = mybir.dt.int32
AF = mybir.ActivationFunctionType
ALU = mybir.AluOpType
AX = mybir.AxisListType
BF = ml_dtypes.bfloat16

B, T, HID = 4, 4096, 2048
NH = 8
DK = 128
C = 128
NTC = T // C
NFC = HID // 128
CH = NH * DK
TK = T // 2
INTER = 8192
NI = INTER // 128
SCALE = DK ** -0.5

TAPS = []
PHASES = ["P1", "P2", "P3", "P4"]
NOCONV = False
DBG = False
DBG2 = False
P3LIM = 9


class KB:

    SEMS = ["ds", "dg", "T", "V", "S", "G", "CC"]

    def __init__(self):
        self.marks = {}
        self.real = False
        self.nc = None
        self.t = {}
        self.ps = {}
        self.sem_h = {}
        self.cnt = {}

    def reset_counts(self):
        self.cnt = {s: 0 for s in self.SEMS}

    def do(self, eng, name, *a, sem=None, n=1, **k):
        i = None
        if self.real:
            i = getattr(eng, name)(*a, **k)
        if sem is not None:
            self.cnt[sem] += n
            if self.real:
                i.then_inc(self.sem_h[sem], n)
        return i

    def mark(self, label, sem):
        if not self.real:
            assert label not in self.marks, f"dup mark {label}"
            self.marks[label] = (sem, self.cnt[sem])

    def wait(self, eng, *labels):
        if self.real:
            for label in labels:
                sem, v = self.marks[label]
                if v > 0:
                    eng.wait_ge(self.sem_h[sem], v)


def build_nc(taps=(), phases=("P1", "P2", "P3", "P4")):
    nc = bass.Bass("TRN2", target_bir_lowering=False,
                   detect_race_conditions=False)
    kb = KB()
    kb.nc = nc
    kb.phases = set(phases)
    t = kb.t

    def ein(n, s, d):
        h = nc.dram_tensor(n, list(s), d, kind="ExternalInput")
        t[n] = h
        return h

    ein("hT", [HID, T], F32)
    ein("hTres", [HID, TK], F32)
    for w in ["Wq", "Wk", "Wv", "Ww"]:
        ein(w, [HID, CH], F32)
    ein("cw", [128, 128], F32)
    ein("Wf1", [HID, 128], F32)
    ein("Wf2", [128, CH], F32)
    ein("Wb", [HID, NH], F32)
    ein("Wg1", [HID, 128], F32)
    ein("Wg2", [128, CH], F32)
    ein("bg2", [1, CH], F32)
    ein("norm_w", [128, 1], F32)
    ein("Wo", [16, 128, HID], F32)
    ein("wg8", [NI, NFC, 128, 128], I8)
    ein("wu8", [NI, NFC, 128, 128], I8)
    ein("wd8", [NFC, NI, 128, 128], I8)
    ein("s_gate", [128, NI], F32)
    ein("s_up", [128, NI], F32)
    ein("s_down", [128, NFC], F32)
    ein("ident", [128, 128], BF16)
    ein("mcs", [128, 128], BF16)
    ein("mcols", [128, 2], BF16)
    ein("mask", [128, 128], F32)
    ein("meta", [1, 64], I32)

    t["outT"] = nc.dram_tensor("outT", [HID, TK], F32, kind="ExternalOutput")

    def din(n, s, d):
        t[n] = nc.dram_tensor(n, list(s), d, kind="Internal")

    for n in ["sp_q", "sp_k", "sp_v", "sp_w"]:
        din(n, [NTC, CH, C], BF16)
    din("sp_ghi", [NTC, C, CH], BF16)
    din("sp_glo", [NTC, C, CH], BF16)
    din("sp_sg", [NTC, C, CH], BF16)
    din("sp_x1", [NFC, 128, TK], BF16)
    din("ost", [2 * 1024, 2048], BF16)
    din("cc", [2, 2048, 2048], BF16)
    din("ccP", [2048, 2048], BF16)
    din("fuse", [NI, 128, TK], BF16)
    for tap in taps:
        th = t[tap]
        t["tap_" + tap] = nc.dram_tensor("tap_" + tap, list(th.shape),
                                         th.dtype, kind="ExternalOutput")
    if DBG:
        t["dbg_ps"] = nc.dram_tensor("dbg_ps", [128, 512], F32,
                                     kind="ExternalOutput")
        t["dbg_stg"] = nc.dram_tensor("dbg_stg", [128, 512], BF16,
                                      kind="ExternalOutput")
    if DBG2:
        for nm2, dt2 in [("vt_s", BF16), ("og_b", BF16), ("ro8", F32),
                         ("orms8", F32),
                         ("ghc0", BF16), ("glc0", BF16), ("qcb0", BF16),
                         ("D_s", F32), ("E_s", F32), ("U_f", F32),
                         ("o_s", F32), ("skf_s", BF16), ("wt_s", BF16),
                         ("kqm_s", BF16), ("ccol", F32), ("comb_s", F32),
                         ("es_s", F32), ("s_s", F32), ("at_s", BF16),
                         ("SvF_b", BF16), ("pmt_s", BF16)]:
            shp = [128, 16 if nm2 == "ccol" else
                   8 if nm2 in ("comb_s", "ro8", "orms8") else
                   512 if nm2 in ("kqm_s", "es_s", "s_s", "at_s", "pmt_s") else 1024]
            t["dbg2_" + nm2] = nc.dram_tensor("dbg2_" + nm2, shp, dt2,
                                              kind="ExternalOutput")

    SB_BASE = 20480
    SB_TOP = 224 * 1024 - 64
    SZ = {BF16: 2, F32: 4, I8: 1, I32: 4}
    off = [SB_BASE]

    def sb(name, shape, dtype, at=None):
        per = int(np.prod(shape[1:])) * SZ[dtype]
        if at is None:
            at = off[0]
            off[0] = at + ((per + 31) // 32 * 32)
        else:
            at = ((at + 31) // 32) * 32
        assert at >= SB_BASE and at + per <= SB_TOP, (name, at, per)
        t[name] = nc.alloc_sbuf_tensor_at(name, list(shape), dtype, offset=at)
        return at + per

    t["meta_s"] = nc.alloc_sbuf_tensor("meta_s", [1, 64], I32)
    sb("ident_s", [128, 128], BF16)
    sb("mcs_s", [128, 128], BF16)
    sb("mcols_s", [128, 2], BF16)
    sb("mask_s", [128, 128], F32)
    sb("ones_r", [1, 128], BF16)
    sb("ones_c", [128, 1], BF16)
    sb("ones_r32", [1, 128], F32)
    sb("wrm", [1, 8], F32)
    sb("cw_s", [128, 128], F32)
    sb("wbS", [128, NFC * 8], BF16)
    sb("beta_t", [128, NTC * NH], BF16)
    sb("nw_s", [128, 1], F32)
    sb("f1_s", [128, T], BF16)
    sb("f1g_s", [128, T], BF16)
    sb("wf2_s", [128, CH], BF16)
    sb("wg2_s", [128, CH], BF16)
    sb("bg2_s", [1, CH], BF16)
    sb("sdown_s", [128, NFC], F32)
    sb("eps6", [128, 1], F32)
    sb("eps5", [128, 1], F32)
    sb("SkT", [128, NH * 128], F32)
    sb("Sv", [128, NH * 128], F32)
    sb("pcol", [128, NH], F32)
    ARENA = off[0]

    def ov(name, shape, dtype, rel):
        return sb(name, shape, dtype, at=ARENA + rel) - ARENA

    a = ov("hT_s", [128, NFC * T], BF16, 0)
    a = ov("wbufA", [128, NFC * 128], BF16, a)
    a = ov("wbufB", [128, NFC * 128], BF16, a)
    a = ov("cbuf", [128, 515], F32, a)
    a = ov("ybuf", [128, 512], F32, a)
    a = ov("stgA", [128, 512], BF16, a)
    a = ov("stgB", [128, 512], BF16, a)
    zoff = a
    a = ov("zbuf", [128, 1024], F32, a)
    a = ov("gbuf", [128, 1024], F32, a)
    a = ov("hibuf", [128, 1024], BF16, a)
    a = ov("lobuf", [128, 1024], BF16, a)
    a = ov("sgbuf", [128, 1024], BF16, a)
    boff = a
    a = ov("betaT", [8, T], BF16, a)
    ov("hstgA", [128, 2048], F32, zoff)
    ov("wstgA", [128, NFC * 128], F32, zoff)
    ov("hstgB", [128, 2048], F32, boff)
    ov("wstgB", [128, NFC * 128], F32, boff)
    if DBG:
        a = ov("dbgv_s", [128, 512], F32, a)

    a2 = 0
    for nm2 in ["qcb", "kcb", "vcb", "wcb", "ghc", "glc", "sgc"]:
        for par in (0, 1):
            a2 = ov(f"{nm2}{par}", [128, 1024], BF16, a2)
    for nm2, dt2 in [("D_s", F32), ("E_s", F32), ("U_f", F32), ("SkF_f", F32),
                     ("SvF_f", F32), ("scrF", F32), ("o_s", F32),
                     ("wt_s", BF16), ("um_s", BF16), ("kt_s", BF16),
                     ("vt_s", BF16), ("skf_s", BF16), ("U_b", BF16),
                     ("SkF_b", BF16), ("SvF_b", BF16), ("og_b", BF16),
                     ("ot0", BF16), ("ot1", BF16)]:
        a2 = ov(nm2, [128, 1024], dt2, a2)
    ov("wo_s", [128, 16 * 2048], BF16, 0)
    ov("x1a", [128, NFC * TK], BF16, 0)
    ov("fusS", [128, NI * 512], BF16, 0)
    ov("oF0", [128, 16 * 512], BF16, 64 * 1024)
    ov("wd_s", [128, NI * 128], BF16, 64 * 1024)
    ov("oF1", [128, 16 * 512], BF16, 80 * 1024)
    ov("wg0", [128, NFC * 128], BF16, 80 * 1024)
    ov("wg1_", [128, NFC * 128], BF16, 84 * 1024)
    ov("wu0", [128, NFC * 128], BF16, 88 * 1024)
    ov("wu1_", [128, NFC * 128], BF16, 92 * 1024)
    ov("x1r0", [128, NFC * 512], BF16, 80 * 1024)
    ov("hres0", [128, NFC * 512], BF16, 96 * 1024)
    ov("x1r1", [128, NFC * 512], BF16, 96 * 1024)
    ov("fstg0", [128, 2048], BF16, 96 * 1024)
    ov("fstg1", [128, 2048], BF16, 100 * 1024)
    ov("gsil", [128, 2048], F32, 104 * 1024)
    a3 = ov("r_f", [128, NFC * 512], F32, 112 * 1024)
    a3 = ov("rb", [128, NFC * 512], BF16, a3)
    ov("x1t", [128, NFC * 512], BF16, 144 * 1024)
    a3 = ov("sqb", [128, 512], BF16, a3)
    a3 = ov("sg8", [128, NI], F32, a3)
    a3 = ov("su8", [128, NI], F32, a3)
    a3 = ov("sd8", [128, NFC], F32, a3)
    a3 = ov("musb", [1, 512], F32, a3)
    a3 = ov("sqsb", [1, 512], F32, a3)
    a3 = ov("var_", [1, 512], F32, a3)
    a3 = ov("rstd", [1, 512], F32, a3)
    for nm2, w2, dt2 in [("s_s", 512, F32), ("sm_s", 512, F32),
                         ("es_s", 512, F32), ("af_s", 512, F32),
                         ("kqm_s", 512, BF16), ("ab_s", 512, BF16),
                         ("at_s", 512, BF16), ("pmt_s", 512, BF16),
                         ("ccol", 16, F32), ("tmp8", 8, F32), ("comb_s", 8, F32),
                         ("ssq8", 8, F32), ("wr8", 8, F32), ("rb8", 8, F32),
                         ("nm8", 8, F32), ("ssm8", 8, F32), ("rcp8", 8, F32),
                         ("orms8", 8, F32), ("ro8", 8, F32)]:
        a2 = ov(nm2, [128, w2], dt2, a2)

    for nm, shape, dt_ in [("psA0", [128, 512], F32), ("psA1", [128, 512], F32),
                           ("psA2", [128, 512], F32), ("psA3", [128, 512], F32),
                           ("psB", [128, 1024], F32), ("psT", [128, 1024], BF16),
                           ("psC", [128, 512], F32)]:
        kb.ps[nm] = nc.alloc_psum_tensor(nm, shape, dt_)

    from contextlib import ExitStack
    with ExitStack() as ctx:
        for s in KB.SEMS:
            kb.sem_h[s] = ctx.enter_context(nc.semaphore("sem_" + s))
        kb.taps = list(taps)

        def emit_all():
            kb.reset_counts()
            E = {"sync": None, "g": None, "t": None, "v": None, "s": None}
            if kb.real:
                nums = sorted(h.num for h in kb.sem_h.values())
                assert nums == list(range(nums[0], nums[0] + len(nums)))
                rng = range(nums[0], nums[-1] + 1)
                nc.gpsimd.dma_reset(rng)
                nc.gpsimd.sem_clear(rng)
                nc.all_engine_barrier()
                E = {"sync": nc.sync, "g": nc.gpsimd, "t": nc.tensor,
                     "v": nc.vector, "s": nc.scalar}
            emit_sync(kb, E["sync"])
            emit_gpsimd(kb, E["g"])
            emit_tensor(kb, E["t"])
            emit_vector(kb, E["v"])
            emit_scalar(kb, E["s"])

        kb.real = False
        emit_all()
        cnt1 = dict(kb.cnt)
        kb.real = True
        emit_all()
        assert kb.cnt == cnt1, (kb.cnt, cnt1)
    return nc


P1W = ["sp_q", "sp_k", "sp_v", "sp_w"]
WSRC = {"sp_q": "Wq", "sp_k": "Wk", "sp_v": "Wv", "sp_w": "Ww"}


def p1_gpsimd(kb, g):
    t = kb.t
    D = lambda *a, **k: kb.do(g, "dma_start", *a, sem="dg", n=16, **k)
    D(t["wf2_s"][:, :], t["Wf2"].ap()[:, :])
    D(t["wg2_s"][:, :], t["Wg2"].ap()[:, :])
    D(t["bg2_s"][:, :], t["bg2"].ap()[:, :])
    D(t["wbS"][:, :].rearrange("p (a b) -> p a b", b=8),
      bass.AP(t["Wb"], 0, [[NH, 128], [NH * 128, NFC], [1, NH]]))
    kb.do(g, "memset", t["ones_r"][:, :], 1.0)
    kb.do(g, "memset", t["ones_c"][:, :], 1.0)
    kb.do(g, "memset", t["ones_r32"][:, :], 1.0)
    kb.do(g, "memset", t["wrm"][:, :], 1.0)
    kb.do(g, "memset", t["eps6"][:, :], 1e-6)
    kb.do(g, "memset", t["eps5"][:, :], 1e-5)
    kb.do(g, "memset", t["SkT"][:, :], 0.0)
    kb.do(g, "memset", t["Sv"][:, :], 0.0)
    kb.do(g, "memset", t["pcol"][:, :], 0.0, sem="G")
    kb.mark("g.consts", "dg")
    kb.mark("g.memset", "G")
    for fh in range(0, 2 * NFC, 2):
        fc, hf = fh // 2, fh % 2
        if fh >= 2:
            kb.wait(g, f"v.hcast.{fh - 2}")
        D(t["hstgA"][:, :], t["hT"].ap()[fc * 128:(fc + 1) * 128,
                                         hf * 2048:(hf + 1) * 2048])
        D(t["meta_s"][:, :], t["meta"].ap()[:, :])
        kb.mark(f"g.hT.{fh}", "dg")
    kb.mark("g.hTsent", "dg")
    for wi in range(4):
        W = t[WSRC[P1W[wi]]]
        for j in range(8):
            wj = wi * 8 + j
            stg = t["wstgA"] if wj % 2 == 0 else t["wstgB"]
            if wj >= 2:
                kb.wait(g, f"v.wcast.{wj - 2}")
            else:
                kb.wait(g, f"v.hcast.{30 + wj}")
            D(stg[:, :].rearrange("p (a b) -> p a b", b=128),
              bass.AP(W, j * 128, [[CH, 128], [128 * CH, NFC], [1, 128]]))
            D(t["meta_s"][:, :], t["meta"].ap()[:, :])
            kb.mark(f"g.p1w.{wj}", "dg")
    kb.mark("g.wsent", "dg")
    D(t["meta_s"][:, :], t["meta"].ap()[:, :])
    kb.mark("g.wsent2", "dg")
    kb.wait(g, "v.wcast.30")
    D(t["wstgA"][:, 0:2048].rearrange("p (a b) -> p a b", b=128),
      bass.AP(t["Wf1"], 0, [[128, 128], [128 * 128, NFC], [1, 128]]))
    kb.mark("g.wf1", "dg")
    kb.wait(g, "v.wcast.31")
    D(t["wstgB"][:, 0:2048].rearrange("p (a b) -> p a b", b=128),
      bass.AP(t["Wg1"], 0, [[128, 128], [128 * 128, NFC], [1, 128]]))
    kb.mark("g.wg1", "dg")
    D(t["meta_s"][:, :], t["meta"].ap()[:, :])
    kb.mark("g.gsent", "dg")
    kb.mark("p1.dg", "dg")
    if DBG:
        kb.wait(g, "v.dbg")
        D(t["dbg_ps"].ap(), t["dbgv_s"][:, :])


def p1_sync(kb, s):
    t = kb.t
    D = lambda *a, **k: kb.do(s, "dma_start", *a, sem="ds", n=16, **k)
    D(t["ident_s"][:, :], t["ident"].ap()[:, :])
    D(t["mcs_s"][:, :], t["mcs"].ap()[:, :])
    D(t["mcols_s"][:, :], t["mcols"].ap()[:, :])
    D(t["mask_s"][:, :], t["mask"].ap()[:, :])
    D(t["meta_s"][:, :], t["meta"].ap()[:, :])
    D(t["cw_s"][:, :], t["cw"].ap()[:, :])
    D(t["nw_s"][:, :], t["norm_w"].ap()[:, :])
    kb.mark("s.consts", "ds")
    for fh in range(1, 2 * NFC, 2):
        fc, hf = fh // 2, fh % 2
        if fh >= 3:
            kb.wait(s, f"v.hcast.{fh - 2}")
        D(t["hstgB"][:, :], t["hT"].ap()[fc * 128:(fc + 1) * 128,
                                         hf * 2048:(hf + 1) * 2048])
        D(t["meta_s"][:, :], t["meta"].ap()[:, :])
        kb.mark(f"s.hT.{fh}", "ds")
    for wi in range(4):
        sp = t[P1W[wi]]
        for j in range(8):
            for tau in range(8):
                idx = (wi * 8 + j) * 8 + tau
                stg = t["stgA"] if idx % 2 == 0 else t["stgB"]
                kb.wait(s, f"sc.p1s.{idx}")
                dst = bass.AP(sp, (tau * 4) * (CH * C) + (j * 128) * C,
                              [[C, 128], [CH * C, 4], [1, C]])
                D(dst, stg[:, :].rearrange("p (a b) -> p a b", b=C))
                kb.mark(f"s.p1s.{idx}", "ds")
    for ts in range(NTC):
        kb.wait(s, f"v.glo.{ts}")
        D(bass.AP(t["sp_ghi"], ts * C * CH, [[CH, C], [1, CH]]), t["hibuf"][:, :])
        D(bass.AP(t["sp_glo"], ts * C * CH, [[CH, C], [1, CH]]), t["lobuf"][:, :])
        kb.mark(f"s.g.{ts}", "ds")
    for ts in range(NTC):
        kb.wait(s, f"sc.sgz.{ts}")
        D(bass.AP(t["sp_sg"], ts * C * CH, [[CH, C], [1, CH]]), t["sgbuf"][:, :])
        kb.mark(f"s.sg.{ts}", "ds")
    kb.mark("p1.ds", "ds")
    if DBG:
        kb.wait(s, "sc.p1s.0")
        D(t["dbg_stg"].ap(), t["stgA"][:, :])


def p1_tensor(kb, e):
    t, ps = kb.t, kb.ps
    MM = lambda *a, **k: kb.do(e, "matmul", *a, sem="T", **k)
    kb.wait(e, "g.consts", "s.consts", "g.memset")
    kb.wait(e, "g.consts", "s.consts")
    for _ in range(144):
        MM(ps["psB"][:, 0:128], t["ident_s"][:, :], t["ident_s"][:, 0:128],
           start=True, stop=True)
    kb.mark("t.warm", "T")
    kb.wait(e, "v.hcast.30", "v.hcast.31")
    for wi in range(4):
        for j in range(8):
            wj = wi * 8 + j
            buf = t["wbufA"] if wj % 2 == 0 else t["wbufB"]
            kb.wait(e, f"v.wcast.{wj}")
            for tau in range(8):
                idx = wj * 8 + tau
                pb = ps[f"psA{idx % 4}"]
                if idx >= 4:
                    kb.wait(e, f"v.p1c.{idx - 4}")
                for fc in range(NFC):
                    MM(pb[:, :], buf[:, fc * 128:(fc + 1) * 128],
                       t["hT_s"][:, fc * T + tau * 512: fc * T + tau * 512 + 512],
                       start=(fc == 0), stop=(fc == NFC - 1))
                kb.mark(f"t.p1m.{idx}", "T")
            kb.mark(f"t.p1w.{wj}", "T")
    kb.wait(e, "v.p1c.last")
    for src, bufn in [(0, "wbufA"), (1, "wbufB")]:
        kb.wait(e, "v.wcast.f" if src == 0 else "v.wcast.g")
        kb.wait(e, "g.wf1" if src == 0 else "g.wg1")
        for tau in range(8):
            fidx = src * 8 + tau
            pb = ps[f"psA{fidx % 2}"]
            if fidx >= 2:
                kb.wait(e, f"v.f1e.{fidx - 2}")
            for fc in range(NFC):
                MM(pb[:, :], t[bufn][:, fc * 128:(fc + 1) * 128],
                   t["hT_s"][:, fc * T + tau * 512: fc * T + tau * 512 + 512],
                   start=(fc == 0), stop=(fc == NFC - 1))
            kb.mark(f"t.f1m.{fidx}", "T")
    kb.wait(e, "v.f1e.15")
    for ts in range(NTC):
        if ts >= 1:
            kb.wait(e, f"sc.gz.{ts - 1}")
        for half in range(2):
            MM(ps["psB"][:, half * 512:(half + 1) * 512],
               t["f1_s"][:, ts * 128:(ts + 1) * 128],
               t["wf2_s"][:, half * 512:(half + 1) * 512],
               start=True, stop=True)
        kb.mark(f"t.gz.{ts}", "T")
    kb.wait(e, "v.f1e.14", "v.f1e.15")
    for tau in range(8):
        if tau >= 2:
            kb.wait(e, f"sc.beta.{tau - 2}")
        pb = ps[f"psA{tau % 2}"]
        for fc in range(NFC):
            MM(pb[0:8, :], t["wbS"][:, fc * 8:(fc + 1) * 8],
               t["hT_s"][:, fc * T + tau * 512: fc * T + tau * 512 + 512],
               start=(fc == 0), stop=(fc == NFC - 1))
        kb.mark(f"t.beta.{tau}", "T")
    kb.wait(e, "sc.gz.31")
    for ts in range(NTC):
        if ts >= 1:
            kb.wait(e, f"sc.sgz.{ts - 1}")
        for half in range(2):
            MM(ps["psB"][:, half * 512:(half + 1) * 512],
               t["f1g_s"][:, ts * 128:(ts + 1) * 128],
               t["wg2_s"][:, half * 512:(half + 1) * 512],
               start=True, stop=False)
            MM(ps["psB"][:, half * 512:(half + 1) * 512],
               t["ones_r"][:, 0:128],
               t["bg2_s"][:, half * 512:(half + 1) * 512],
               start=False, stop=True)
        kb.mark(f"t.sgz.{ts}", "T")
    kb.wait(e, "sc.beta.7")
    for ts in range(NTC):
        if ts % 8 == 0 and ts > 0:
            kb.wait(e, f"v.betat.{ts // 8 - 1}")
        kb.do(e, "transpose", ps["psT"][:, (ts % 8) * 8:(ts % 8) * 8 + 8],
              t["betaT"][:, ts * 128:(ts + 1) * 128], t["ident_s"][0:8, 0:8],
              sem="T")
        if ts % 8 == 7:
            kb.mark(f"t.betat.{ts // 8}", "T")
    kb.mark("p1.T", "T")


def p1_casts(kb, e):
    t = kb.t
    for fh in range(2 * NFC):
        fc, hf = fh // 2, fh % 2
        stg = t["hstgA"] if fh % 2 == 0 else t["hstgB"]
        kb.wait(e, f"g.hT.{fh}" if fh % 2 == 0 else f"s.hT.{fh}")
        kb.do(e, "tensor_copy",
              t["hT_s"][:, fc * T + hf * 2048: fc * T + hf * 2048 + 2048],
              stg[:, :], sem="V")
        kb.mark(f"v.hcast.{fh}", "V")


def p1_vector(kb, e):
    t, ps = kb.t, kb.ps
    kb.wait(e, "s.consts")
    p1_casts(kb, e)
    if DBG:
        kb.wait(e, "t.p1m.0")
        kb.do(e, "tensor_copy", t["dbgv_s"][:, :], ps["psA0"][:, :], sem="V")
        kb.mark("v.dbg", "V")
    for wi in range(4):
        for j in range(8):
            wj = wi * 8 + j
            stg = t["wstgA"] if wj % 2 == 0 else t["wstgB"]
            buf = t["wbufA"] if wj % 2 == 0 else t["wbufB"]
            kb.wait(e, f"g.p1w.{wj}")
            if wj >= 2:
                kb.wait(e, f"t.p1w.{wj - 2}")
            kb.do(e, "tensor_copy", buf[:, :], stg[:, :], sem="V")
            kb.mark(f"v.wcast.{wj}", "V")
            for tau in range(8):
                idx = wj * 8 + tau
                pb = ps[f"psA{idx % 4}"]
                kb.wait(e, f"t.p1m.{idx}")
                if idx >= 1:
                    kb.wait(e, f"sc.p1s.{idx - 1}")
                if NOCONV:
                    kb.wait(e, f"sc.p1s.{idx}")
                    kb.do(e, "engine_nop", sem="V")
                    kb.mark(f"v.p1c.{idx}", "V")
                    continue
                if tau == 0:
                    kb.do(e, "memset", t["cbuf"][:, 0:3], 0.0)
                kb.do(e, "tensor_copy", t["cbuf"][:, 3:515], pb[:, :])
                cw = t["cw_s"]
                cwb = wi * 32 + j * 4
                kb.do(e, "tensor_scalar_mul", t["ybuf"][:, :],
                      t["cbuf"][:, 3:515], cw[:, cwb + 3:cwb + 4])
                for i in (2, 1, 0):
                    kb.do(e, "scalar_tensor_tensor", t["ybuf"][:, :],
                          t["cbuf"][:, i:i + 512], cw[:, cwb + i:cwb + i + 1],
                          t["ybuf"][:, :], ALU.mult, ALU.add)
                kb.do(e, "tensor_copy", t["cbuf"][:, 0:3], t["cbuf"][:, 512:515],
                      sem="V")
                kb.mark(f"v.p1c.{idx}", "V")
    kb.mark("v.p1c.last", "V")
    for src, bufn, lbl in [(0, "wbufA", "g.wg1"), (1, "wbufB", "g.gsent")]:
        kb.wait(e, lbl)
        kb.wait(e, "v.p1c.last")
        stg = t["wstgA"] if src == 0 else t["wstgB"]
        kb.do(e, "tensor_copy", t[bufn][:, 0:2048], stg[:, 0:2048], sem="V")
        kb.mark("v.wcast.f" if src == 0 else "v.wcast.g", "V")
    for src, outn in [(0, "f1_s"), (1, "f1g_s")]:
        for tau in range(8):
            fidx = src * 8 + tau
            pb = ps[f"psA{fidx % 2}"]
            kb.wait(e, f"t.f1m.{fidx}")
            kb.do(e, "tensor_copy", t[outn][:, tau * 512:(tau + 1) * 512],
                  pb[:, :], sem="V")
            kb.mark(f"v.f1e.{fidx}", "V")
    for ts in range(NTC):
        kb.wait(e, f"sc.gn.{ts}")
        if ts >= 1:
            kb.wait(e, f"s.g.{ts - 1}")
        kb.do(e, "tensor_copy", t["hibuf"][:, :], t["gbuf"][:, :], sem="V")
        kb.mark(f"v.ghi.{ts}", "V")
        kb.wait(e, f"v.ghi.{ts}")
        kb.do(e, "tensor_tensor", t["lobuf"][:, :], t["gbuf"][:, :],
              t["hibuf"][:, :], ALU.subtract, sem="V")
        kb.mark(f"v.glo.{ts}", "V")
    for g8 in range(NTC // 8):
        kb.wait(e, f"t.betat.{g8}")
        kb.do(e, "tensor_copy", t["beta_t"][:, g8 * 64:(g8 + 1) * 64],
              ps["psT"][:, 0:64], sem="V")
        kb.mark(f"v.betat.{g8}", "V")
    kb.mark("p1.V", "V")


def load_act_set(kb, e, idx):
    if kb.real:
        e.add_instruction(mybir.InstLoadActFuncSet(
            name=kb.nc.get_next_instruction_name(), act_func_set_id=idx,
            ins=[], outs=[]))


def act_warmup(kb, e):
    t = kb.t
    kb.wait(e, "g.memset")
    for idx, fn in [(6, AF.Exp), (2, AF.Sigmoid), (18, AF.Silu)]:
        load_act_set(kb, e, idx)
        kb.do(e, "activation", t["wrm"][0:1, 0:8], t["wrm"][0:1, 0:8], fn)


def p1_scalar(kb, e):
    t, ps = kb.t, kb.ps
    act_warmup(kb, e)
    for wi in range(4):
        for j in range(8):
            for tau in range(8):
                idx = (wi * 8 + j) * 8 + tau
                stg = t["stgA"] if idx % 2 == 0 else t["stgB"]
                pb = ps[f"psA{idx % 4}"]
                if NOCONV:
                    kb.wait(e, f"t.p1m.{idx}")
                else:
                    kb.wait(e, f"v.p1c.{idx}")
                if idx >= 2:
                    kb.wait(e, f"s.p1s.{idx - 2}")
                kb.do(e, "activation", stg[:, :],
                      pb[:, :] if NOCONV else t["ybuf"][:, :],
                      AF.Identity if NOCONV else AF.Silu,
                      sem="S")
                kb.mark(f"sc.p1s.{idx}", "S")
    kb.wait(e, "v.wcast.f", "v.wcast.g")
    load_act_set(kb, e, 6)
    kb.do(e, "activation", t["wrm"][0:1, 0:8], t["wrm"][0:1, 0:8], AF.Exp)
    for ts in range(NTC):
        kb.wait(e, f"t.gz.{ts}")
        kb.do(e, "activation", t["zbuf"][:, :], ps["psB"][:, :], AF.Exp,
              scale=-1.0, sem="S")
        kb.mark(f"sc.gz.{ts}", "S")
        if ts >= 1:
            kb.wait(e, f"v.glo.{ts - 1}")
        kb.do(e, "activation", t["gbuf"][:, :], t["zbuf"][:, :], AF.Ln,
              bias=1.0, sem="S")
        kb.mark(f"sc.gn.{ts}", "S")
    load_act_set(kb, e, 2)
    kb.do(e, "activation", t["wrm"][0:1, 0:8], t["wrm"][0:1, 0:8], AF.Sigmoid)
    for tau in range(8):
        pb = ps[f"psA{tau % 2}"]
        kb.wait(e, f"t.beta.{tau}")
        kb.do(e, "activation", t["betaT"][:, tau * 512:(tau + 1) * 512],
              pb[0:8, :], AF.Sigmoid, sem="S")
        kb.mark(f"sc.beta.{tau}", "S")
    for ts in range(NTC):
        kb.wait(e, f"t.sgz.{ts}")
        if ts >= 1:
            kb.wait(e, f"s.sg.{ts - 1}")
        kb.do(e, "activation", t["sgbuf"][:, :], ps["psB"][:, :], AF.Sigmoid,
              sem="S")
        kb.mark(f"sc.sgz.{ts}", "S")
        if ts == NTC - 1:
            kb.mark("p1.S", "S")



SPF = {"sp_q": "qcb", "sp_k": "kcb", "sp_v": "vcb", "sp_w": "wcb"}


def p2_sync(kb, s):
    t = kb.t
    D = lambda *a, **k: kb.do(s, "dma_start", *a, sem="ds", n=16, **k)
    kb.wait(s, "p1.T", "p1.V", "p1.S", "p1.ds", "p1.dg")
    for tau in range(NTC):
        p = tau % 2
        if tau >= 2:
            kb.wait(s, f"c{tau - 2}.tlast", f"c{tau - 2}.vlast")
        for spn, bufn in SPF.items():
            D(t[f"{bufn}{p}"][:, :].rearrange("p (h u) -> p h u", u=C),
              bass.AP(t[spn], tau * CH * C, [[C, 128], [128 * C, NH], [1, C]]))
        D(t[f"ghc{p}"][:, :], bass.AP(t["sp_ghi"], tau * C * CH, [[CH, C], [1, CH]]))
        D(t[f"glc{p}"][:, :], bass.AP(t["sp_glo"], tau * C * CH, [[CH, C], [1, CH]]))
        D(t[f"sgc{p}"][:, :], bass.AP(t["sp_sg"], tau * C * CH, [[CH, C], [1, CH]]))
        D(t["meta_s"][:, :], t["meta"].ap()[:, :])
        kb.mark(f"c{tau}.ld", "ds")
        if DBG2 and tau == 0:
            kb.wait(s, "c0.vlast", "c0.g1.st")
            for nm2 in ["vt_s", "og_b", "ro8", "orms8",
                        "ghc0", "glc0", "qcb0",
                        "D_s", "E_s", "U_f", "o_s", "skf_s", "wt_s", "kqm_s",
                        "ccol", "comb_s", "es_s", "s_s", "at_s", "SvF_b",
                        "pmt_s"]:
                D(t["dbg2_" + nm2].ap()[:, :], t[nm2][:, :])
            kb.mark("dbg2", "ds")
    kb.mark("p2.end.sync", "ds")


def p2_gpsimd(kb, g):
    t = kb.t
    kb.wait(g, "p1.T", "p1.V", "p1.S", "p1.ds", "p1.dg")
    for tau in range(NTC):
        kb.wait(g, f"c{tau}.ote")
        off = (tau // 16) * (1024 * 2048) + (tau % 16) * 128
        kb.do(g, "dma_start",
              bass.AP(t["ost"], off, [[2048, 128], [128 * 2048, NH], [1, 128]]),
              t[f"ot{tau % 2}"][:, :].rearrange("p (h u) -> p h u", u=128),
              sem="dg", n=16)
        kb.mark(f"c{tau}.ost", "dg")
        if tau == 15:
            kb.wait(g, "c15.ost")
            kb.do(g, "collective_compute", "AllGather", ALU.bypass,
                  [[0, 1], [2, 3], [4, 5], [6, 7]],
                  [bass.AP(t["ost"], 0, [[2048, 1024], [1, 2048]])],
                  [bass.AP(t["cc"], 0, [[2048, 2048], [1, 2048]])], sem="CC")
            kb.mark("ag.a", "CC")
        if tau == 31:
            kb.wait(g, "c31.ost")
            kb.do(g, "collective_compute", "AllGather", ALU.bypass,
                  [[0, 1], [2, 3], [4, 5], [6, 7]],
                  [bass.AP(t["ost"], 1024 * 2048, [[2048, 1024], [1, 2048]])],
                  [bass.AP(t["cc"], 2048 * 2048, [[2048, 2048], [1, 2048]])],
                  sem="CC")
            kb.mark("ag.b", "CC")
    kb.wait(g, "ag.a", "ag.b")
    if kb.real:
        reg = g.alloc_register("r_ccp")
        g.reg_load(reg, t["meta_s"][0:1, 1:2])
        g.dma_start(bass.AP(t["ccP"], 0, [[2048, 2048], [1, 2048]]),
                    bass.AP(t["cc"], reg, [[2048, 2048], [1, 2048]])
                    ).then_inc(kb.sem_h["dg"], 16)
    kb.cnt["dg"] += 16
    kb.mark("ccp", "dg")


def p2_tensor(kb, e):
    t, ps = kb.t, kb.ps
    MM = lambda *a, **k: kb.do(e, "matmul", *a, sem="T", **k)
    TR = lambda o, i: kb.do(e, "transpose", o, i, t["ident_s"][:, :], sem="T")
    kb.wait(e, "p1.T", "p1.V", "p1.S", "p1.ds", "p1.dg")
    for tau in range(NTC):
        p = tau % 2
        gh, gl = t[f"ghc{p}"], t[f"glc{p}"]
        kb.wait(e, f"c{tau}.ld")
        if DBG2 and tau == 1:
            kb.wait(e, "dbg2")
        if tau >= 1:
            kb.wait(e, f"c{tau - 1}.g1.st")
        for half in range(2):
            MM(ps["psB"][:, half * 512:(half + 1) * 512], t["mcs_s"][:, :],
               gh[:, half * 512:(half + 1) * 512], start=True, stop=False)
            MM(ps["psB"][:, half * 512:(half + 1) * 512], t["mcs_s"][:, :],
               gl[:, half * 512:(half + 1) * 512], start=False, stop=True)
        kb.mark(f"c{tau}.cs", "T")
        if tau >= 1:
            kb.wait(e, f"c{tau - 1}.ccol")
        for h in range(NH):
            MM(ps["psC"][:, h * 2:h * 2 + 2], gh[:, h * 128:(h + 1) * 128],
               t["mcols_s"][:, :], start=True, stop=False)
            MM(ps["psC"][:, h * 2:h * 2 + 2], gl[:, h * 128:(h + 1) * 128],
               t["mcols_s"][:, :], start=False, stop=True)
        kb.mark(f"c{tau}.cols", "T")
        if tau >= 1:
            kb.wait(e, f"c{tau - 1}.ote")
        for h in range(NH):
            TR(ps["psT"][:, h * 128:(h + 1) * 128], t[f"wcb{p}"][:, h * 128:(h + 1) * 128])
        kb.mark(f"c{tau}.r1", "T")
        kb.wait(e, f"c{tau}.r1e", f"c{tau}.Ub")
        for h in range(NH):
            TR(ps["psT"][:, h * 128:(h + 1) * 128], t["U_b"][:, h * 128:(h + 1) * 128])
        kb.mark(f"c{tau}.r2", "T")
        kb.wait(e, f"c{tau}.r2e")
        for h in range(NH):
            TR(ps["psT"][:, h * 128:(h + 1) * 128], t[f"kcb{p}"][:, h * 128:(h + 1) * 128])
        kb.mark(f"c{tau}.r3", "T")
        kb.wait(e, f"c{tau}.r3e")
        for h in range(NH):
            TR(ps["psT"][:, h * 128:(h + 1) * 128], t[f"vcb{p}"][:, h * 128:(h + 1) * 128])
        kb.mark(f"c{tau}.r4", "T")
        kb.wait(e, f"c{tau}.r4e", f"c{tau}.SkFb")
        for h in range(NH):
            TR(ps["psT"][:, h * 128:(h + 1) * 128], t["SkF_b"][:, h * 128:(h + 1) * 128])
        kb.mark(f"c{tau}.r5", "T")
        for G in range(2):
            hs = range(G * 4, G * 4 + 4)
            if (tau, G) != (0, 0):
                pt, pg = (tau, 0) if G == 1 else (tau - 1, 1)
                kb.wait(e, f"c{pt}.g{pg}.kqme")
            for h in hs:
                MM(ps["psA0"][:, (h % 4) * 128:(h % 4 + 1) * 128],
                   t[f"kcb{p}"][:, h * 128:(h + 1) * 128],
                   t[f"qcb{p}"][:, h * 128:(h + 1) * 128], start=True, stop=True)
            kb.mark(f"c{tau}.g{G}.kq", "T")
            if (tau, G) != (0, 0):
                pt, pg = (tau, 0) if G == 1 else (tau - 1, 1)
                kb.wait(e, f"c{pt}.g{pg}.sev")
            kb.wait(e, f"c{tau}.r5e" if G == 0 else f"c{tau}.r5e",
                    f"c{tau}.g{G}.kqme")
            for h in hs:
                MM(ps["psA1"][:, (h % 4) * 128:(h % 4 + 1) * 128],
                   t[f"qcb{p}"][:, h * 128:(h + 1) * 128],
                   t["skf_s"][:, h * 128:(h + 1) * 128], start=True, stop=False)
                MM(ps["psA1"][:, (h % 4) * 128:(h % 4 + 1) * 128],
                   t["kqm_s"][:, (h % 4) * 128:(h % 4 + 1) * 128],
                   t["U_b"][:, h * 128:(h + 1) * 128], start=False, stop=True)
            kb.mark(f"c{tau}.g{G}.ms", "T")
            kb.wait(e, f"c{tau}.r5e" if G == 0 else f"c{tau}.g0.ate")
            kb.wait(e, f"c{tau}.g{G}.ab")
            for h in hs:
                TR(ps["psT"][:, (h % 4) * 128 + G * 512:(h % 4 + 1) * 128 + G * 512],
                   t["ab_s"][:, (h % 4) * 128:(h % 4 + 1) * 128])
            kb.mark(f"c{tau}.g{G}.att", "T")
            if (tau, G) != (0, 0):
                pt, pg = (tau, 0) if G == 1 else (tau - 1, 1)
                kb.wait(e, f"c{pt}.g{pg}.pmte")
            kb.wait(e, f"c{tau}.g{G}.ate")
            for h in hs:
                MM(ps["psA2"][:, (h % 4) * 128:(h % 4 + 1) * 128],
                   t["um_s"][:, h * 128:(h + 1) * 128],
                   t["at_s"][:, (h % 4) * 128:(h % 4 + 1) * 128],
                   start=True, stop=True)
            kb.mark(f"c{tau}.g{G}.mp", "T")
            if (tau, G) != (0, 0):
                pt, pg = (tau, 0) if G == 1 else (tau - 1, 1)
                kb.wait(e, f"c{pt}.g{pg}.oe")
            kb.wait(e, f"c{tau}.g{G}.pmte", f"c{tau}.SvFb")
            for h in hs:
                MM(ps["psA3"][:, (h % 4) * 128:(h % 4 + 1) * 128],
                   t["at_s"][:, (h % 4) * 128:(h % 4 + 1) * 128],
                   t["SvF_b"][:, h * 128:(h + 1) * 128], start=True, stop=False)
                MM(ps["psA3"][:, (h % 4) * 128:(h % 4 + 1) * 128],
                   t["pmt_s"][:, (h % 4) * 128:(h % 4 + 1) * 128],
                   t["vt_s"][:, h * 128:(h + 1) * 128], start=False, stop=True)
            kb.mark(f"c{tau}.g{G}.mo", "T")
            kb.wait(e, f"c{tau}.E" if G == 0 else f"c{tau}.g0.st")
            for h in hs:
                MM(ps["psB"][:, (h % 4) * 128:(h % 4 + 1) * 128],
                   t["U_b"][:, h * 128:(h + 1) * 128],
                   t["kt_s"][:, h * 128:(h + 1) * 128], start=True, stop=True)
                MM(ps["psB"][:, 512 + (h % 4) * 128:512 + (h % 4 + 1) * 128],
                   t["U_b"][:, h * 128:(h + 1) * 128],
                   t["vt_s"][:, h * 128:(h + 1) * 128], start=True, stop=True)
            kb.mark(f"c{tau}.g{G}.ukuv", "T")
        kb.wait(e, f"c{tau}.g1.ate", f"c{tau}.ogb")
        for h in range(NH):
            TR(ps["psT"][:, h * 128:(h + 1) * 128], t["og_b"][:, h * 128:(h + 1) * 128])
        kb.mark(f"c{tau}.r6", "T")
        kb.mark(f"c{tau}.tlast", "T")
    kb.mark("p2.end.T", "T")


def p2_vector(kb, e):
    t, ps = kb.t, kb.ps
    V = lambda *a, **k: kb.do(e, *a, sem="V", **k)
    kb.wait(e, "p1.T", "p1.V", "p1.S", "p1.ds", "p1.dg")
    b3 = lambda buf, w=8: bass.AP(t[buf], 0, [[w, 128], [1, w], [0, 128]])
    for tau in range(NTC):
        p = tau % 2
        kb.wait(e, f"c{tau}.cols")
        V("tensor_copy", t["ccol"][:, :], ps["psC"][:, 0:16])
        kb.mark(f"c{tau}.ccol", "V")
        kb.wait(e, f"c{tau}.ccol")
        V("tensor_tensor", t["tmp8"][:, :], t["pcol"][:, :],
          bass.AP(t["ccol"], 1, [[16, 128], [2, 8]]), ALU.add)
        kb.mark(f"c{tau}.addc", "V")
        V("tensor_copy", t["pcol"][:, :], bass.AP(t["ccol"], 0, [[16, 128], [2, 8]]))
        kb.wait(e, f"c{tau}.r1")
        V("tensor_copy", t["wt_s"][:, :], ps["psT"][:, :])
        kb.mark(f"c{tau}.r1e", "V")
        kb.wait(e, f"c{tau}.wsq")
        V("reduce_sum", t["ssq8"][:, :],
          t["scrF"][:, :].rearrange("p (h u) -> p h u", u=128), axis=AX.X)
        kb.mark(f"c{tau}.ssq", "V")
        kb.wait(e, f"c{tau}.wr")
        V("tensor_tensor", t["rb8"][:, :], t["wr8"][:, :],
          t["beta_t"][:, tau * 8:(tau + 1) * 8], ALU.mult)
        kb.mark(f"c{tau}.rb8", "V")
        kb.wait(e, f"c{tau}.E", f"c{tau}.rb8")
        V("tensor_tensor", t["U_f"][:, :], t["wt_s"][:, :], t["E_s"][:, :],
          ALU.mult)
        V("tensor_tensor", t["U_f"][:, :].rearrange("p (h u) -> p h u", u=128),
          t["U_f"][:, :].rearrange("p (h u) -> p h u", u=128), b3("rb8"),
          ALU.mult)
        V("tensor_copy", t["U_b"][:, :], t["U_f"][:, :])
        kb.mark(f"c{tau}.Ub", "V")
        kb.wait(e, f"c{tau}.comb")
        V("tensor_tensor", t["SkF_f"][:, :].rearrange("p (h u) -> p h u", u=128),
          t["SkT"][:, :].rearrange("p (h u) -> p h u", u=128), b3("comb_s"),
          ALU.mult)
        V("tensor_copy", t["SkF_b"][:, :], t["SkF_f"][:, :])
        kb.mark(f"c{tau}.SkFb", "V")
        V("tensor_tensor", t["SvF_f"][:, :].rearrange("p (h u) -> p h u", u=128),
          t["Sv"][:, :].rearrange("p (h u) -> p h u", u=128), b3("comb_s"),
          ALU.mult)
        V("tensor_copy", t["SvF_b"][:, :], t["SvF_f"][:, :])
        kb.mark(f"c{tau}.SvFb", "V")
        for rr, dst in [("r2", "um_s"), ("r3", "kt_s"), ("r4", "vt_s"),
                        ("r5", "skf_s")]:
            kb.wait(e, f"c{tau}.{rr}")
            V("tensor_copy", t[dst][:, :], ps["psT"][:, :])
            kb.mark(f"c{tau}.{rr}e", "V")
        for G in range(2):
            gsl = slice(G * 512, (G + 1) * 512)
            kb.wait(e, f"c{tau}.g{G}.kq")
            V("tensor_tensor", t["kqm_s"][:, :].rearrange("p (h u) -> p h u", u=128),
              ps["psA0"][:, :].rearrange("p (h u) -> p h u", u=128),
              bass.AP(t["mask_s"], 0, [[128, 128], [0, 4], [1, 128]]), ALU.mult)
            kb.mark(f"c{tau}.g{G}.kqme", "V")
            kb.wait(e, f"c{tau}.g{G}.ms")
            V("scalar_tensor_tensor", t["s_s"][:, :], ps["psA1"][:, :], SCALE,
              t["D_s"][:, gsl], ALU.mult, ALU.mult)
            kb.mark(f"c{tau}.g{G}.sev", "V")
            V("reduce_max", t["nm8"][:, G * 4:G * 4 + 4],
              t["s_s"][:, :].rearrange("p (h u) -> p h u", u=128), axis=AX.X,
              negate=True)
            kb.mark(f"c{tau}.g{G}.nm", "V")
            kb.wait(e, f"c{tau}.g{G}.nm")
            V("tensor_tensor", t["sm_s"][:, :].rearrange("p (h u) -> p h u", u=128),
              t["s_s"][:, :].rearrange("p (h u) -> p h u", u=128),
              bass.AP(t["nm8"], G * 4, [[8, 128], [1, 4], [0, 128]]), ALU.add)
            kb.mark(f"c{tau}.g{G}.sm", "V")
            kb.wait(e, f"c{tau}.g{G}.es")
            V("reduce_sum", t["ssm8"][:, G * 4:G * 4 + 4],
              t["es_s"][:, :].rearrange("p (h u) -> p h u", u=128), axis=AX.X)
            kb.mark(f"c{tau}.g{G}.ssum", "V")
            kb.wait(e, f"c{tau}.g{G}.rcp")
            V("tensor_tensor", t["af_s"][:, :].rearrange("p (h u) -> p h u", u=128),
              t["es_s"][:, :].rearrange("p (h u) -> p h u", u=128),
              bass.AP(t["rcp8"], G * 4, [[8, 128], [1, 4], [0, 128]]), ALU.mult)
            V("tensor_tensor", t["af_s"][:, :], t["af_s"][:, :], t["D_s"][:, gsl],
              ALU.mult)
            V("tensor_copy", t["ab_s"][:, :], t["af_s"][:, :])
            kb.mark(f"c{tau}.g{G}.ab", "V")
            kb.wait(e, f"c{tau}.g{G}.att")
            V("tensor_copy", t["at_s"][:, :], ps["psT"][:, gsl])
            kb.mark(f"c{tau}.g{G}.ate", "V")
            kb.wait(e, f"c{tau}.g{G}.mp")
            V("tensor_tensor", t["pmt_s"][:, :].rearrange("p (h u) -> p h u", u=128),
              ps["psA2"][:, :].rearrange("p (h u) -> p h u", u=128),
              bass.AP(t["mask_s"], 0, [[128, 128], [0, 4], [1, 128]]), ALU.mult)
            kb.mark(f"c{tau}.g{G}.pmte", "V")
            kb.wait(e, f"c{tau}.g{G}.mo")
            V("tensor_copy", t["o_s"][:, gsl], ps["psA3"][:, :])
            kb.mark(f"c{tau}.g{G}.oe", "V")
            kb.wait(e, f"c{tau}.g{G}.ukuv")
            V("tensor_tensor", t["SkT"][:, gsl], t["SkF_f"][:, gsl],
              ps["psB"][:, 0:512], ALU.add)
            V("tensor_tensor", t["Sv"][:, gsl], t["SvF_f"][:, gsl],
              ps["psB"][:, 512:1024], ALU.add)
            kb.mark(f"c{tau}.g{G}.st", "V")
        kb.wait(e, f"c{tau}.osq")
        V("reduce_sum", t["orms8"][:, :],
          t["scrF"][:, :].rearrange("p (h u) -> p h u", u=128), axis=AX.X)
        kb.mark(f"c{tau}.orms", "V")
        kb.wait(e, f"c{tau}.ro")
        V("tensor_tensor", t["o_s"][:, :].rearrange("p (h u) -> p h u", u=128),
          t["o_s"][:, :].rearrange("p (h u) -> p h u", u=128), b3("ro8"),
          ALU.mult)
        V("tensor_tensor", t["o_s"][:, :], t["o_s"][:, :], t[f"sgc{p}"][:, :],
          ALU.mult)
        V("tensor_copy", t["og_b"][:, :], t["o_s"][:, :])
        kb.mark(f"c{tau}.ogb", "V")
        kb.wait(e, f"c{tau}.r6")
        if tau >= 2:
            kb.wait(e, f"c{tau - 2}.ost")
        V("tensor_copy", t[f"ot{tau % 2}"][:, :], ps["psT"][:, :])
        kb.mark(f"c{tau}.ote", "V")
        kb.mark(f"c{tau}.vlast", "V")
    kb.mark("p2.end.V", "V")


def p2_scalar(kb, e):
    t, ps = kb.t, kb.ps
    S = lambda *a, **k: kb.do(e, "activation", *a, sem="S", **k)
    kb.wait(e, "p1.T", "p1.V", "p1.S", "p1.ds", "p1.dg")
    load_act_set(kb, e, 6)
    kb.do(e, "activation", t["wrm"][0:1, 0:8], t["wrm"][0:1, 0:8], AF.Exp)
    for tau in range(NTC):
        kb.wait(e, f"c{tau}.cs")
        S(t["D_s"][:, :], ps["psB"][:, :], AF.Exp)
        S(t["E_s"][:, :], ps["psB"][:, :], AF.Exp, scale=-1.0)
        kb.mark(f"c{tau}.E", "S")
        kb.wait(e, f"c{tau}.addc")
        S(t["comb_s"][:, :], t["tmp8"][:, :], AF.Exp)
        kb.mark(f"c{tau}.comb", "S")
        kb.wait(e, f"c{tau}.r1e")
        S(t["scrF"][:, :], t["wt_s"][:, :], AF.Square)
        kb.mark(f"c{tau}.wsq", "S")
        kb.wait(e, f"c{tau}.ssq")
        S(t["wr8"][:, :], t["ssq8"][:, :], AF.Ln, bias=t["eps6"][:, :])
        S(t["wr8"][:, :], t["wr8"][:, :], AF.Exp, scale=-0.5)
        kb.mark(f"c{tau}.wr", "S")
        for G in range(2):
            kb.wait(e, f"c{tau}.g{G}.sm")
            S(t["es_s"][:, :], t["sm_s"][:, :], AF.Exp)
            kb.mark(f"c{tau}.g{G}.es", "S")
            kb.wait(e, f"c{tau}.g{G}.ssum")
            S(t["rcp8"][:, G * 4:G * 4 + 4], t["ssm8"][:, G * 4:G * 4 + 4],
              AF.Ln)
            S(t["rcp8"][:, G * 4:G * 4 + 4], t["rcp8"][:, G * 4:G * 4 + 4],
              AF.Exp, scale=-1.0)
            kb.mark(f"c{tau}.g{G}.rcp", "S")
        kb.wait(e, f"c{tau}.g1.oe", f"c{tau}.ssq")
        S(t["scrF"][:, :], t["o_s"][:, :], AF.Square)
        kb.mark(f"c{tau}.osq", "S")
        kb.wait(e, f"c{tau}.orms")
        S(t["ro8"][:, :], t["orms8"][:, :], AF.Ln, scale=1.0 / 128.0,
          bias=t["eps5"][:, :])
        S(t["ro8"][:, :], t["ro8"][:, :], AF.Exp, scale=-0.5)
        kb.mark(f"c{tau}.ro", "S")
    kb.mark("p2.end.S", "S")



P2END = ("p2.end.sync", "p2.end.T", "p2.end.V", "p2.end.S")


def p3_gpsimd(kb, g):
    t = kb.t
    D = lambda *a, **k: kb.do(g, "dma_start", *a, sem="dg", n=16, **k)
    kb.wait(g, *P2END)
    for hc in range(16):
        D(t["wo_s"][:, hc * 2048:(hc + 1) * 2048].rearrange(
            "p (a b) -> p a b", b=128),
          bass.AP(t["Wo"], hc * 128 * HID, [[HID, 128], [128, NFC], [1, 128]]))
        kb.mark(f"p3.wo.{hc}", "dg")
    D(t["meta_s"][:, :], t["meta"].ap()[:, :])
    kb.mark("p3.wosent", "dg")
    for tt in range(4):
        if tt >= 1:
            kb.wait(g, f"p3.radd.{tt - 1}")
        D(t["hres0"][:, :].rearrange("p (a b) -> p a b", b=512),
          bass.AP(t["hTres"], tt * 512, [[TK, 128], [128 * TK, NFC], [1, 512]]))
        kb.mark(f"p3.hres.{tt}", "dg")
        D(t["meta_s"][:, :], t["meta"].ap()[:, :])
        kb.mark(f"p3.hsentt.{tt}", "dg")
    D(t["meta_s"][:, :], t["meta"].ap()[:, :])
    kb.mark("p3.hsent", "dg")


def p3_sync(kb, s):
    t = kb.t
    D = lambda *a, **k: kb.do(s, "dma_start", *a, sem="ds", n=16, **k)
    kb.wait(s, *P2END)
    kb.wait(s, "ccp")
    for tt in range(4):
        if tt < 2:
            D(t[f"oF{tt % 2}"][:, :].rearrange("p (a b) -> p a b", b=512),
              bass.AP(t["ccP"], tt * 512, [[2048, 128], [128 * 2048, 16], [1, 512]]))
            kb.mark(f"p3.oF.{tt}", "ds")
    for tt in range(4):
        kb.wait(s, f"p3.x1.{tt}")
        D(bass.AP(t["sp_x1"], tt * 512, [[TK, 128], [128 * TK, NFC], [1, 512]]),
          t["x1t"][:, :].rearrange("p (a b) -> p a b", b=512))
        kb.mark(f"p3.x1sp.{tt}", "ds")
        if tt + 2 < 4:
            kb.wait(s, f"p3.mm.{tt}")
            D(t[f"oF{tt % 2}"][:, :].rearrange("p (a b) -> p a b", b=512),
              bass.AP(t["ccP"], (tt + 2) * 512,
                      [[2048, 128], [128 * 2048, 16], [1, 512]]))
            kb.mark(f"p3.oF.{tt + 2}", "ds")
        if tt == 1:
            D(t["meta_s"][:, :], t["meta"].ap()[:, :])
            kb.mark("p3.osent", "ds")
    kb.mark("p3.end.sync", "ds")


def p3_tensor(kb, e):
    t, ps = kb.t, kb.ps
    MM = lambda *a, **k: kb.do(e, "matmul", *a, sem="T", **k)
    kb.wait(e, *P2END)
    kb.wait(e, "p3.wosent", "v.nwfold")
    for tt in range(4):
        oF = t[f"oF{tt % 2}"]
        kb.wait(e, f"p3.oF.{tt + 1}" if tt < 3 else "p3.osent")
        for fc in range(NFC):
            pb = ps[f"psA{fc % 2}"]
            if (tt, fc) > (0, 1):
                pfc = fc - 2 if fc >= 2 else NFC - 2 + fc
                ptt = tt if fc >= 2 else tt - 1
                kb.wait(e, f"p3.radd.{ptt}.{pfc}")
            if P3LIM >= 1:
                for hc in range(16):
                    MM(pb[:, :], t["wo_s"][:, hc * 2048 + fc * 128:hc * 2048 + fc * 128 + 128],
                       oF[:, hc * 512:(hc + 1) * 512],
                       start=(hc == 0), stop=(hc == 15))
            kb.mark(f"p3.mmf.{tt}.{fc}", "T")
        kb.mark(f"p3.mm.{tt}", "T")
        for fc in range(NFC):
            kb.wait(e, f"p3.rb.{tt}.{fc}", f"p3.sqb.{tt}.{fc}")
            if fc == 0 and tt > 0:
                kb.wait(e, f"p3.x1.{tt - 1}")
            if P3LIM >= 2:
                MM(ps["psC"][0:1, :], t["ones_c"][:, :],
                   t["rb"][:, fc * 512:(fc + 1) * 512],
                   start=(fc == 0), stop=(fc == NFC - 1))
                MM(ps["psA2"][0:1, :], t["ones_c"][:, :], t["sqb"][:, :],
                   start=(fc == 0), stop=(fc == NFC - 1))
            kb.mark(f"p3.smm.{tt}.{fc}", "T")
        kb.mark(f"p3.stat.{tt}", "T")
        kb.wait(e, f"p3.rstd.{tt}")
        if P3LIM >= 3:
            MM(ps["psB"][:, 0:512], t["ones_r32"][:, :], t["musb"][0:1, :],
               start=True, stop=True)
            MM(ps["psB"][:, 512:1024], t["ones_r32"][:, :], t["rstd"][0:1, :],
               start=True, stop=True)
        kb.mark(f"p3.bc.{tt}", "T")
    kb.mark("p3.end.T", "T")


def p3_vector(kb, e):
    t, ps = kb.t, kb.ps
    V = lambda *a, **k: kb.do(e, *a, sem="V", **k)
    kb.wait(e, *P2END)
    for hc in range(16):
        kb.wait(e, f"p3.wo.{hc + 1}" if hc < 15 else "p3.wosent")
        V("tensor_scalar_mul", t["r_f"][:, 0:2048],
          t["wo_s"][:, hc * 2048:(hc + 1) * 2048], t["nw_s"][:, :])
        kb.do(e, "engine_nop", sem="V")
        kb.mark(f"v.nwtmp.{hc}", "V")
        kb.wait(e, f"v.nwtmp.{hc}")
        V("tensor_copy", t["wo_s"][:, hc * 2048:(hc + 1) * 2048],
          t["r_f"][:, 0:2048])
    kb.mark("v.nwfold", "V")
    for tt in range(4):
        hres = t["hres0"]
        kb.wait(e, f"p3.hsentt.{tt}")
        if tt > 0:
            kb.wait(e, f"p3.x1sp.{tt - 1}")
        for fc in range(NFC):
            pb = ps[f"psA{fc % 2}"]
            kb.wait(e, f"p3.mmf.{tt}.{fc}")
            V("tensor_tensor", t["r_f"][:, fc * 512:(fc + 1) * 512],
              pb[:, :], hres[:, fc * 512:(fc + 1) * 512], ALU.add)
            kb.mark(f"p3.radd.{tt}.{fc}", "V")
            V("tensor_copy", t["rb"][:, fc * 512:(fc + 1) * 512],
              t["r_f"][:, fc * 512:(fc + 1) * 512])
            kb.mark(f"p3.rb.{tt}.{fc}", "V")
        kb.mark(f"p3.radd.{tt}", "V")
        kb.wait(e, f"p3.musq.{tt}")
        V("tensor_tensor", t["var_"][0:1, :], t["sqsb"][0:1, :],
          t["var_"][0:1, :], ALU.subtract)
        kb.mark(f"p3.var.{tt}", "V")
        kb.wait(e, f"p3.bc.{tt}")
        if tt > 0:
            kb.wait(e, f"p3.x1sp.{tt - 1}")
        for fc in range(NFC):
            V("tensor_tensor", t["r_f"][:, fc * 512:(fc + 1) * 512],
              t["r_f"][:, fc * 512:(fc + 1) * 512], ps["psB"][:, 0:512],
              ALU.subtract)
            V("tensor_tensor", t["x1t"][:, fc * 512:(fc + 1) * 512],
              t["r_f"][:, fc * 512:(fc + 1) * 512], ps["psB"][:, 512:1024],
              ALU.mult)
        kb.mark(f"p3.x1.{tt}", "V")
    kb.mark("p3.end.V", "V")


def p3_scalar(kb, e):
    t, ps = kb.t, kb.ps
    kb.wait(e, *P2END)
    for tt in range(4):
        for fc in range(NFC):
            kb.wait(e, f"p3.rb.{tt}.{fc}")
            if fc > 0:
                kb.wait(e, f"p3.smm.{tt}.{fc - 1}")
            kb.do(e, "activation", t["sqb"][:, :],
                  t["r_f"][:, fc * 512:(fc + 1) * 512], AF.Square, sem="S")
            kb.mark(f"p3.sqb.{tt}.{fc}", "S")
        kb.wait(e, f"p3.stat.{tt}")
        kb.do(e, "activation", t["musb"][0:1, :], ps["psC"][0:1, :],
              AF.Copy, scale=1.0 / HID)
        kb.do(e, "activation", t["sqsb"][0:1, :], ps["psA2"][0:1, :],
              AF.Copy, scale=1.0 / HID, sem="S")
        kb.mark(f"p3.mus.{tt}", "S")
        kb.wait(e, f"p3.mus.{tt}")
        kb.do(e, "activation", t["var_"][0:1, :], t["musb"][0:1, :],
              AF.Square, sem="S")
        kb.mark(f"p3.musq.{tt}", "S")
        kb.wait(e, f"p3.var.{tt}")
        kb.do(e, "activation", t["rstd"][0:1, :], t["var_"][0:1, :], AF.Ln,
              bias=t["eps5"][0:1, :])
        kb.do(e, "activation", t["rstd"][0:1, :], t["rstd"][0:1, :], AF.Exp,
              scale=-0.5, sem="S")
        kb.mark(f"p3.rstd.{tt}", "S")
    kb.mark("p3.end.S", "S")


P3END = ("p3.end.sync", "p3.end.T", "p3.end.V", "p3.end.S")


def p4_gpsimd(kb, g):
    t = kb.t
    D = lambda *a, **k: kb.do(g, "dma_start", *a, sem="dg", n=16, **k)
    kb.wait(g, *P3END)
    for fc in range(NFC):
        D(t["x1a"][:, fc * TK:(fc + 1) * TK],
          bass.AP(t["sp_x1"], fc * 128 * TK, [[TK, 128], [1, TK]]))
    D(t["meta_s"][:, :], t["meta"].ap()[:, :])
    kb.mark("p4.x1a", "dg")
    D(t["sg8"][:, :], t["s_gate"].ap()[:, :])
    D(t["su8"][:, :], t["s_up"].ap()[:, :])
    D(t["sd8"][:, :], t["s_down"].ap()[:, :])
    D(t["meta_s"][:, :], t["meta"].ap()[:, :])
    kb.mark("p4.scales", "dg")
    for i in range(NI):
        if i >= 2:
            kb.wait(g, f"p4.gu.{i - 2}")
        D(t["wg0" if i % 2 == 0 else "wg1_"][:, :].rearrange(
            "p (a b) -> p a b", b=128),
          bass.AP(t["wg8"], i * NFC * 128 * 128, [[128, 128], [128 * 128, NFC], [1, 128]]))
        D(t["wu0" if i % 2 == 0 else "wu1_"][:, :].rearrange(
            "p (a b) -> p a b", b=128),
          bass.AP(t["wu8"], i * NFC * 128 * 128, [[128, 128], [128 * 128, NFC], [1, 128]]))
        D(t["meta_s"][:, :], t["meta"].ap()[:, :])
        kb.mark(f"p4.ws.{i}", "dg")
    kb.mark("p4.wsent", "dg")
    for tb in range(4):
        for fc in range(NFC):
            wdi = tb * NFC + fc
            if wdi >= 1:
                kb.wait(g, f"p4.dmmA.{wdi - 1}")
            D(t["wd_s"][:, 0:32 * 128].rearrange("p (a b) -> p a b", b=128),
              bass.AP(t["wd8"], fc * NI * 128 * 128,
                      [[128, 128], [128 * 128, 32], [1, 128]]))
            D(t["meta_s"][:, :], t["meta"].ap()[:, :])
            kb.mark(f"p4.wdsA.{wdi}", "dg")
            if wdi >= 1:
                kb.wait(g, f"p4.dmm.{wdi - 1}")
            D(t["wd_s"][:, 32 * 128:].rearrange("p (a b) -> p a b", b=128),
              bass.AP(t["wd8"], fc * NI * 128 * 128 + 32 * 128 * 128,
                      [[128, 128], [128 * 128, 32], [1, 128]]))
            D(t["meta_s"][:, :], t["meta"].ap()[:, :])
            kb.mark(f"p4.wdsB.{wdi}", "dg")
    kb.mark("p4.dsent", "dg")


def p4_sync(kb, s):
    t = kb.t
    D = lambda *a, **k: kb.do(s, "dma_start", *a, sem="ds", n=16, **k)
    kb.wait(s, *P3END)
    for i in range(NI):
        kb.wait(s, f"p4.fev.{i}")
        D(bass.AP(t["fuse"], i * 128 * TK, [[TK, 128], [1, TK]]),
          t[f"fstg{i % 2}"][:, :])
        kb.mark(f"p4.fuse.{i}", "ds")
    SENT = lambda: D(t["meta_s"][:, :], t["meta"].ap()[:, :])
    for tb in range(4):
        kb.wait(s, f"p4.fuse.{NI - 1}")
        if tb >= 1:
            kb.wait(s, f"p4.dmmA.{tb * NFC - 1}")
        D(t["fusS"][:, 0:32 * 512].rearrange("p (a b) -> p a b", b=512),
          bass.AP(t["fuse"], tb * 512, [[TK, 128], [128 * TK, 32], [1, 512]]))
        SENT()
        kb.mark(f"p4.fusA.{tb}", "ds")
        if tb >= 1:
            kb.wait(s, f"p4.dmm.{tb * NFC - 1}")
        D(t["fusS"][:, 32 * 512:].rearrange("p (a b) -> p a b", b=512),
          bass.AP(t["fuse"], 32 * 128 * TK + tb * 512,
                  [[TK, 128], [128 * TK, 32], [1, 512]]))
        SENT()
        kb.mark(f"p4.fusB.{tb}", "ds")
        if tb >= 2:
            kb.wait(s, f"p4.out.{tb - 2}")
        D(t[f"x1r{tb % 2}"][:, :].rearrange("p (a b) -> p a b", b=512),
          bass.AP(t["sp_x1"], tb * 512, [[TK, 128], [128 * TK, NFC], [1, 512]]))
        SENT()
        kb.mark(f"p4.x1r.{tb}", "ds")
        if tb >= 1:
            for fc in range(NFC):
                kb.wait(s, f"p4.x2.{tb - 1}.{fc}")
                D(bass.AP(t["outT"], (tb - 1) * 512 + fc * 128 * TK,
                          [[TK, 128], [1, 512]]),
                  t["r_f"][:, fc * 512:(fc + 1) * 512])
                kb.mark(f"p4.out.{tb - 1}.{fc}", "ds")
            kb.mark(f"p4.out.{tb - 1}", "ds")
    for fc in range(NFC):
        kb.wait(s, f"p4.x2.3.{fc}")
        D(bass.AP(t["outT"], 3 * 512 + fc * 128 * TK, [[TK, 128], [1, 512]]),
          t["r_f"][:, fc * 512:(fc + 1) * 512])
        kb.mark(f"p4.out.3.{fc}", "ds")
    kb.mark("p4.out.3", "ds")
    kb.mark("p4.end.sync", "ds")


def p4_tensor(kb, e):
    t, ps = kb.t, kb.ps
    MM = lambda *a, **k: kb.do(e, "matmul", *a, sem="T", **k)
    kb.wait(e, *P3END)
    kb.wait(e, "p4.x1a", "p4.scales")
    for i in range(NI):
        kb.wait(e, f"p4.ws.{i}")
        for tb in range(4):
            pg = ps[f"psA{tb % 2}"]
            pu = ps[f"psA{2 + tb % 2}"]
            if (i, tb) > (0, 1):
                pi, ptb = (i, tb - 2) if tb >= 2 else (i - 1, tb + 2)
                kb.wait(e, f"p4.fev.{pi}.{ptb}")
            wgb = t["wg0" if i % 2 == 0 else "wg1_"]
            wub = t["wu0" if i % 2 == 0 else "wu1_"]
            for fc in range(NFC):
                MM(pg[:, :], wgb[:, fc * 128:(fc + 1) * 128],
                   t["x1a"][:, fc * TK + tb * 512: fc * TK + tb * 512 + 512],
                   start=(fc == 0), stop=(fc == NFC - 1))
                MM(pu[:, :], wub[:, fc * 128:(fc + 1) * 128],
                   t["x1a"][:, fc * TK + tb * 512: fc * TK + tb * 512 + 512],
                   start=(fc == 0), stop=(fc == NFC - 1))
            kb.mark(f"p4.gumm.{i}.{tb}", "T")
        kb.mark(f"p4.gu.{i}", "T")
    for tb in range(4):
        for fc in range(NFC):
            wdi = tb * NFC + fc
            kb.wait(e, f"p4.wdsA.{wdi}")
            if fc == 0:
                kb.wait(e, f"p4.fusA.{tb}")
            pb = ps["psA3" if wdi % 3 == 2 else f"psA{wdi % 3}"]
            if wdi >= 3:
                kb.wait(e, f"p4.dsc.{wdi - 3}")
            for i in range(32):
                MM(pb[:, :], t["wd_s"][:, i * 128:(i + 1) * 128],
                   t["fusS"][:, i * 512:(i + 1) * 512],
                   start=(i == 0), stop=False)
            kb.mark(f"p4.dmmA.{wdi}", "T")
            kb.wait(e, f"p4.wdsB.{wdi}")
            if fc == 0:
                kb.wait(e, f"p4.fusB.{tb}")
            for i in range(32, NI):
                MM(pb[:, :], t["wd_s"][:, i * 128:(i + 1) * 128],
                   t["fusS"][:, i * 512:(i + 1) * 512],
                   start=False, stop=(i == NI - 1))
            kb.mark(f"p4.dmm.{wdi}", "T")
        for fc in range(NFC):
            kb.wait(e, f"p4.rb2.{tb}.{fc}", f"p4.sqb2.{tb}.{fc}")
            if fc == 0 and tb > 0:
                kb.wait(e, f"p4.x2.{tb - 1}")
            MM(ps["psC"][0:1, :], t["ones_c"][:, :],
               t["rb"][:, fc * 512:(fc + 1) * 512],
               start=(fc == 0), stop=(fc == NFC - 1))
            MM(ps["psA2"][0:1, :], t["ones_c"][:, :], t["sqb"][:, :],
               start=(fc == 0), stop=(fc == NFC - 1))
            kb.mark(f"p4.smm.{tb}.{fc}", "T")
        kb.mark(f"p4.stat.{tb}", "T")
        kb.wait(e, f"p4.rstd.{tb}")
        MM(ps["psB"][:, 0:512], t["ones_r32"][:, :], t["musb"][0:1, :],
           start=True, stop=True)
        MM(ps["psB"][:, 512:1024], t["ones_r32"][:, :], t["rstd"][0:1, :],
           start=True, stop=True)
        kb.mark(f"p4.bc.{tb}", "T")
    kb.mark("p4.end.T", "T")


def p4_vector(kb, e):
    t, ps = kb.t, kb.ps
    V = lambda *a, **k: kb.do(e, *a, sem="V", **k)
    kb.wait(e, *P3END)
    for i in range(NI):
        for tb in range(4):
            pu = ps[f"psA{2 + tb % 2}"]
            kb.wait(e, f"p4.gumm.{i}.{tb}", f"p4.sil.{i}.{tb}")
            if i >= 2:
                kb.wait(e, f"p4.fuse.{i - 2}")
            V("scalar_tensor_tensor",
              t[f"fstg{i % 2}"][:, tb * 512:(tb + 1) * 512], pu[:, :],
              t["su8"][:, i:i + 1], t["gsil"][:, tb * 512:(tb + 1) * 512],
              ALU.mult, ALU.mult)
            kb.mark(f"p4.fev.{i}.{tb}", "V")
        kb.mark(f"p4.fev.{i}", "V")
    for tb in range(4):
        x1r = t[f"x1r{tb % 2}"]
        for fc in range(NFC):
            wdi = tb * NFC + fc
            pb = ps[f"psA{fc % 2}"]
            kb.wait(e, f"p4.dsc.{wdi}")
            V("tensor_tensor", t["r_f"][:, fc * 512:(fc + 1) * 512],
              t["r_f"][:, fc * 512:(fc + 1) * 512],
              x1r[:, fc * 512:(fc + 1) * 512], ALU.add)
            kb.mark(f"p4.dev.{wdi}", "V")
            V("tensor_copy", t["rb"][:, fc * 512:(fc + 1) * 512],
              t["r_f"][:, fc * 512:(fc + 1) * 512])
            kb.mark(f"p4.rb2.{tb}.{fc}", "V")
        kb.wait(e, f"p4.musq.{tb}")
        V("tensor_tensor", t["var_"][0:1, :], t["sqsb"][0:1, :],
          t["var_"][0:1, :], ALU.subtract)
        kb.mark(f"p4.var.{tb}", "V")
        kb.wait(e, f"p4.bc.{tb}")
        if tb > 0:
            kb.wait(e, f"p4.out.{tb - 1}")
        for fc in range(NFC):
            V("tensor_tensor", t["r_f"][:, fc * 512:(fc + 1) * 512],
              t["r_f"][:, fc * 512:(fc + 1) * 512], ps["psB"][:, 0:512],
              ALU.subtract)
            V("tensor_tensor", t["r_f"][:, fc * 512:(fc + 1) * 512],
              t["r_f"][:, fc * 512:(fc + 1) * 512], ps["psB"][:, 512:1024],
              ALU.mult)
            kb.mark(f"p4.x2.{tb}.{fc}", "V")
        kb.mark(f"p4.x2.{tb}", "V")
    kb.mark("p4.end.V", "V")


def p4_scalar(kb, e):
    t, ps = kb.t, kb.ps
    kb.wait(e, *P3END)
    load_act_set(kb, e, 18)
    kb.do(e, "activation", t["wrm"][0:1, 0:8], t["wrm"][0:1, 0:8], AF.Silu)
    for i in range(NI):
        for tb in range(4):
            pg = ps[f"psA{tb % 2}"]
            kb.wait(e, f"p4.gumm.{i}.{tb}")
            if i >= 1:
                kb.wait(e, f"p4.fev.{i - 1}.{tb}")
            kb.do(e, "activation", t["gsil"][:, tb * 512:(tb + 1) * 512],
                  pg[:, :], AF.Silu, scale=t["sg8"][:, i:i + 1], sem="S")
            kb.mark(f"p4.sil.{i}.{tb}", "S")
    load_act_set(kb, e, 6)
    kb.do(e, "activation", t["wrm"][0:1, 0:8], t["wrm"][0:1, 0:8], AF.Exp)
    for tb in range(4):
        for fc in range(NFC):
            wdi = tb * NFC + fc
            pb = ps["psA3" if wdi % 3 == 2 else f"psA{wdi % 3}"]
            kb.wait(e, f"p4.dmm.{wdi}")
            if tb > 0:
                kb.wait(e, f"p4.out.{tb - 1}.{fc}")
            kb.do(e, "activation", t["r_f"][:, fc * 512:(fc + 1) * 512],
                  pb[:, :], AF.Copy, scale=t["sd8"][:, fc:fc + 1], sem="S")
            kb.mark(f"p4.dsc.{wdi}", "S")
        for fc in range(NFC):
            kb.wait(e, f"p4.rb2.{tb}.{fc}")
            if fc > 0:
                kb.wait(e, f"p4.smm.{tb}.{fc - 1}")
            kb.do(e, "activation", t["sqb"][:, :],
                  t["r_f"][:, fc * 512:(fc + 1) * 512], AF.Square, sem="S")
            kb.mark(f"p4.sqb2.{tb}.{fc}", "S")
        kb.wait(e, f"p4.stat.{tb}")
        kb.do(e, "activation", t["musb"][0:1, :], ps["psC"][0:1, :], AF.Copy,
              scale=1.0 / HID)
        kb.do(e, "activation", t["sqsb"][0:1, :], ps["psA2"][0:1, :], AF.Copy,
              scale=1.0 / HID, sem="S")
        kb.mark(f"p4.mus.{tb}", "S")
        kb.wait(e, f"p4.mus.{tb}")
        kb.do(e, "activation", t["var_"][0:1, :], t["musb"][0:1, :],
              AF.Square, sem="S")
        kb.mark(f"p4.musq.{tb}", "S")
        kb.wait(e, f"p4.var.{tb}")
        kb.do(e, "activation", t["rstd"][0:1, :], t["var_"][0:1, :], AF.Ln,
              bias=t["eps5"][0:1, :])
        kb.do(e, "activation", t["rstd"][0:1, :], t["rstd"][0:1, :], AF.Exp,
              scale=-0.5, sem="S")
        kb.mark(f"p4.rstd.{tb}", "S")
    kb.mark("p4.end.S", "S")


def emit_gpsimd(kb, g):
    if "P1" in kb.phases:
        p1_gpsimd(kb, g)
    if "P2" in kb.phases:
        p2_gpsimd(kb, g)
    if "P3" in kb.phases:
        p3_gpsimd(kb, g)
    if "P4" in kb.phases:
        p4_gpsimd(kb, g)
    kb.wait(g, "end.sync")
    if "P2" in kb.phases:
        kb.wait(g, "c31.ost", "ag.b", "ccp")
    for tap in kb.taps:
        th, to = kb.t[tap], kb.t["tap_" + tap]
        kb.do(g, "dma_start", to.ap().opt(), th.ap().opt(), sem="dg", n=16)
    kb.mark("g.taps", "dg")
    kb.wait(g, "g.taps")


def emit_sync(kb, s):
    if "P1" in kb.phases:
        p1_sync(kb, s)
    if "P2" in kb.phases:
        p2_sync(kb, s)
    if "P3" in kb.phases:
        p3_sync(kb, s)
    if "P4" in kb.phases:
        p4_sync(kb, s)
    kb.mark("end.sync", "ds")


def emit_tensor(kb, e):
    if "P1" in kb.phases:
        p1_tensor(kb, e)
    if "P2" in kb.phases:
        p2_tensor(kb, e)
    if "P3" in kb.phases:
        p3_tensor(kb, e)
    if "P4" in kb.phases:
        p4_tensor(kb, e)


def emit_vector(kb, e):
    if "P1" in kb.phases:
        p1_vector(kb, e)
    if "P2" in kb.phases:
        p2_vector(kb, e)
    if "P3" in kb.phases:
        p3_vector(kb, e)
    if "P4" in kb.phases:
        p4_vector(kb, e)


def emit_scalar(kb, e):
    if "P1" in kb.phases:
        p1_scalar(kb, e)
    if "P2" in kb.phases:
        p2_scalar(kb, e)
    if "P3" in kb.phases:
        p3_scalar(kb, e)
    if "P4" in kb.phases:
        p4_scalar(kb, e)


_CACHE = {}


def _consts():
    ident = np.eye(128, dtype=np.float32).astype(BF)
    s = np.arange(128)[:, None]
    tt = np.arange(128)[None, :]
    mcs = ((s <= 63).astype(np.float32) - (s <= tt).astype(np.float32)).astype(BF)
    mcols = np.stack([-(np.arange(128) >= 64).astype(np.float32),
                      -(np.arange(128) <= 63).astype(np.float32)], 1).astype(BF)
    mask = (s <= tt).astype(np.float32)
    return ident, mcs, mcols, mask


def make_in_maps(ip):
    h = ip["hidden_states"].astype(np.float32, copy=False)
    ident, mcs, mcols, mask = _consts()
    wg8 = np.ascontiguousarray(
        ip["w_gate"].reshape(NFC, 128, NI, 128).transpose(2, 0, 1, 3))
    wu8 = np.ascontiguousarray(
        ip["w_up"].reshape(NFC, 128, NI, 128).transpose(2, 0, 1, 3))
    wd8 = np.ascontiguousarray(
        ip["w_down"].reshape(NI, 128, NFC, 128).transpose(2, 0, 1, 3))
    Wo = ip["Wo"].reshape(16, 128, HID)
    in_maps = []
    for c in range(8):
        b, hh = c // 2, c % 2
        cs = slice(hh * CH, (hh + 1) * CH)
        cw = np.zeros((128, 128), np.float32)
        for wi, cname in enumerate(["cq", "ck", "cv", "cv"]):
            cwh = ip[cname][cs].reshape(8, 128, 4)
            for j in range(8):
                cw[:, wi * 32 + j * 4: wi * 32 + j * 4 + 4] = cwh[j]
        meta = np.zeros((1, 64), np.int32)
        meta[0, 0] = hh * TK
        meta[0, 1] = hh * (2048 * 2048)
        hTb = np.ascontiguousarray(h[b].T)
        m = {
            "hT": hTb,
            "hTres": np.ascontiguousarray(hTb[:, hh * TK:(hh + 1) * TK]),
            "Wq": np.ascontiguousarray(ip["Wq"][:, cs]),
            "Wk": np.ascontiguousarray(ip["Wk"][:, cs]),
            "Wv": np.ascontiguousarray(ip["Wv"][:, cs]),
            "Ww": np.ascontiguousarray(ip["Ww"][:, cs]),
            "cw": cw,
            "Wf1": np.ascontiguousarray(ip["Wf1"]),
            "Wf2": np.ascontiguousarray(ip["Wf2"][:, cs]),
            "Wb": np.ascontiguousarray(ip["Wb"][:, hh * NH:(hh + 1) * NH]),
            "Wg1": np.ascontiguousarray(ip["Wg1"]),
            "Wg2": np.ascontiguousarray(ip["Wg2"][:, cs]),
            "bg2": np.ascontiguousarray(ip["bg2"][None, cs]),
            "norm_w": np.ascontiguousarray(ip["norm_w"][:, None]),
            "Wo": np.ascontiguousarray(Wo),
            "wg8": wg8, "wu8": wu8, "wd8": wd8,
            "s_gate": np.ascontiguousarray(ip["s_gate"].reshape(NI, 128).T),
            "s_up": np.ascontiguousarray(ip["s_up"].reshape(NI, 128).T),
            "s_down": np.ascontiguousarray(ip["s_down"].reshape(NFC, 128).T),
            "ident": ident, "mcs": mcs, "mcols": mcols, "mask": mask,
            "meta": meta,
        }
        in_maps.append(m)
    return in_maps


def kernel(**inputs):
    ip = {k.replace("L0_attn_", "").replace("L0_ffn_", ""): np.asarray(v)
          for k, v in inputs.items()}
    key = (tuple(TAPS), tuple(PHASES))
    if key not in _CACHE:
        _CACHE[key] = build_nc(taps=TAPS, phases=PHASES)
    nc = _CACHE[key]
    in_maps = make_in_maps(ip)
    res = run_bass_kernel_spmd(nc, in_maps, core_ids=list(range(8)))
    out = np.zeros((B, T, HID), np.float32)
    for c in range(8):
        b, hh = c // 2, c % 2
        out[b, hh * TK:(hh + 1) * TK, :] = res.results[c]["outT"].T
    return out
